# version 05
# baseline (speedup 1.0000x reference)
import numpy as np
import concourse.bass as bass
import concourse.mybir as mybir

F32 = mybir.dt.float32
BF16 = mybir.dt.bfloat16
F8 = mybir.dt.float8e4
Alu = mybir.AluOpType
Act = mybir.ActivationFunctionType
AX = mybir.AxisListType.X

ENGINES = ("sync", "vector", "scalar", "tensor", "gpsimd")


def fmaj(ap2d, ncols=None):
    return ap2d.rearrange("(j pp) t -> j pp t", pp=128).transpose([1, 0, 2])


def f3(tile_ap, n, c):
    return tile_ap.rearrange("p (j t) -> p j t", t=c)

D = 2048
S = 4096
FC = D // 128
NH = 16
HD = 128
GS = 256
NW = S // GS
DI = 6144
FCI = DI // 128
CH = 512
NCH = S // CH
EPS = 1e-5


class P:

    def __init__(self, nc):
        self.nc = nc
        self.q = {e: [] for e in ENGINES}
        self.sems = {}
        self.cnt = {}

    def sem(self, name):
        if name not in self.sems:
            self.sems[name] = self.nc.alloc_semaphore(name)
            self.cnt[name] = 0
        return self.sems[name]

    def op(self, eng, meth, *args, inc=(), **kw):
        if inc and isinstance(inc[0], str):
            inc = (inc,)
        handles = []
        for s, n in inc:
            self.sem(s)
            self.cnt[s] += n
            handles.append((self.sems[s], n))

        def thunk(e, meth=meth, args=args, kw=kw, handles=handles):
            ins = getattr(e, meth)(*args, **kw)
            for h, n in handles:
                ins = ins.then_inc(h, n)

        self.q[eng].append(thunk)
        return self.cnt[inc[0][0]] if inc else None

    def dma(self, eng, out, in_, sem=None, **kw):
        semname = sem or ("d_" + eng)
        return self.op(eng, "dma_start", inc=((semname, 16),), out=out, in_=in_, **kw)

    def wait(self, eng, semname, val=None):
        self.sem(semname)
        v = self.cnt[semname] if val is None else val
        if v <= 0:
            return
        h = self.sems[semname]
        self.q[eng].append(lambda e, h=h, v=v: e.wait_ge(h, v))

    def now(self, semname):
        self.sem(semname)
        return self.cnt[semname]

    def barrier(self, engines=ENGINES, producers=None):
        prods = producers if producers is not None else list(self.cnt)
        vals = [(s, self.cnt[s]) for s in prods]
        for e in engines:
            for s, v in vals:
                if v > 0:
                    self.wait(e, s, v)

    def flush(self):
        with self.nc.Block() as block:
            for e in ENGINES:
                thunks = self.q[e]
                if not thunks:
                    continue
                getattr(block, e)(lambda eng, thunks=thunks: [t(eng) for t in thunks])
            self.q = {e: [] for e in ENGINES}


LADDER_TMPS = ("ls", "lt", "lz", "la1", "la2", "la3", "la4", "lp", "lr", "lb0")


def emit_e4m3(p, eng, out, x, tmp):
    s, t, z, a1, a2, a3, a4, pp, r, b0 = (tmp[k] for k in LADDER_TMPS)
    o = lambda m, *a: p.op(eng, m, *a)
    o("tensor_single_scalar", t, x, -512.0, Alu.mult)
    o("scalar_tensor_tensor", s, x, 512.0, t, Alu.mult, Alu.max)
    o("tensor_single_scalar", a1, s, 16.0, Alu.is_ge)
    o("tensor_single_scalar", a2, s, 32.0, Alu.is_ge)
    o("tensor_single_scalar", a3, s, 64.0, Alu.is_ge)
    o("tensor_single_scalar", a4, s, 128.0, Alu.is_ge)
    o("tensor_copy", z, s)
    for a in (a1, a2, a3, a4):
        o("tensor_scalar", pp, a, -0.5, 1.0, Alu.mult, Alu.add)
        o("tensor_mul", z, z, pp)
    o("tensor_copy", r, z)
    for thr, w in ((15.5, 16.0), (7.5, 8.0), (3.5, 4.0), (1.5, 2.0)):
        o("tensor_scalar", t, r, thr, w, Alu.is_ge, Alu.mult)
        o("tensor_sub", r, r, t)
    o("tensor_scalar", b0, r, 0.5, 1.0, Alu.is_ge, Alu.mult)
    o("tensor_sub", r, r, b0)
    o("tensor_sub", r, z, r)
    o("tensor_sub", t, r, z)
    o("tensor_single_scalar", t, t, 0.5, Alu.is_equal)
    o("tensor_mul", t, t, b0)
    o("tensor_sub", r, r, t)
    for a in (a1, a2, a3, a4):
        o("tensor_scalar", pp, a, 1.0, None, Alu.add)
        o("tensor_mul", r, r, pp)
    o("tensor_single_scalar", r, r, 2.0 ** -9, Alu.mult)
    o("tensor_scalar", t, x, 0.0, 2.0, Alu.is_ge, Alu.mult)
    o("tensor_mul", t, t, r)
    o("tensor_sub", out, t, r)


def build_nc(stages=("s0", "s1"), debug=()):
    nc = bass.Bass("TRN2", target_bir_lowering=False,
                   detect_race_conditions=False)
    T = {}

    def dram(name, shape, dtype, kind="Internal"):
        T[name] = nc.dram_tensor(name, shape, dtype, kind=kind).ap()
        return T[name]

    scope_guards = []

    def sbuf(name, shape, dtype):
        T[name] = nc.alloc_sbuf_tensor(name, shape, dtype).ap()
        return T[name]

    def sbuf_scoped(name, shape, dtype):
        g = nc.sbuf_tensor(name, shape, dtype)
        h = g.__enter__()
        scope_guards.append(g)
        T[name] = h.ap()
        return T[name]

    def free_scope():
        while scope_guards:
            scope_guards.pop().__exit__(None, None, None)

    def psum(name, shape):
        T[name] = nc.alloc_psum_tensor(name, shape, F32).ap()
        return T[name]

    dram("x", [S, D], F32, "ExternalInput")
    dram("c", [1, D], F32, "ExternalInput")
    dram("wmod", [D, 6 * D], F32, "ExternalInput")
    for w in ("wq", "wk", "wv"):
        dram(w, [D, D], F32, "ExternalInput")
    for b in ("bq", "bk", "bv"):
        dram(b, [1, D], F32, "ExternalInput")
    dram("gw", [D, DI], F32, "ExternalInput")
    dram("uw", [D, DI], F32, "ExternalInput")
    dram("dw", [DI, D], F32, "ExternalInput")
    dram("gsc", [1, DI], F32, "ExternalInput")
    dram("usc", [1, DI], F32, "ExternalInput")
    dram("dsc", [1, D], F32, "ExternalInput")
    dram("ident", [128, 128], F32, "ExternalInput")
    dram("caus", [128, 2 * GS], F32, "ExternalInput")
    dram("out", [S, D], F32, "ExternalOutput")

    def scratch(name, shape, dtype):
        dram(name, shape, dtype, "ExternalOutput" if name in debug else "Internal")

    scratch("xf", [D, S], F32)
    scratch("qf", [D, S], F32)
    scratch("kf", [D, S], F32)
    scratch("vf", [S, D], F32)
    scratch("x2", [D, S], F32)
    scratch("h2h", [D, S], BF16)
    scratch("h2l", [D, S], BF16)
    scratch("fsh", [DI, S], BF16)
    scratch("fsl", [DI, S], BF16)
    scratch("h1f", [D, S], F32)
    scratch("gtmp", [16, 128], F32)

    sbuf("identsb", [128, 128], F32)
    sbuf("onesf", [128, 128], F32)
    sbuf("modsb", [128, 96], F32)
    sbuf("vecs", [128, 6 * FC], F32)
    sbuf("sclT", [128, 2 * FCI + FC], F32)
    sbuf("gbrep", [128, D], F32)
    sbuf("onesr", [1, CH], F32)
    sbuf("consts", [128, 2], F32)

    for i in range(8):
        psum(f"ps{i}", [128, 512])

    p = P(nc)
    if "s0" in stages:
        build_s0(p, T, sbuf_scoped)
        free_scope()
    if "s1" in stages:
        build_s1(p, T, sbuf_scoped)
        free_scope()
    if "s2" in stages:
        build_s2(p, T, sbuf_scoped)
        free_scope()
    if "s3" in stages:
        build_s3(p, T, sbuf_scoped)
        free_scope()
    if "s4" in stages:
        build_s4(p, T, sbuf_scoped)
        free_scope()
    if "s5" in stages:
        build_s5(p, T, sbuf_scoped)
        free_scope()
    if "s6" in stages:
        build_s6(p, T, sbuf_scoped)
        free_scope()

    p.barrier(engines=("sync",))
    p.flush()
    return nc


def build_s0(p, T, sbuf):
    c16 = sbuf("c16", [16, 128], F32)
    siluc = sbuf("siluc", [128, 16], F32)
    wmt = sbuf("wmt", [128, 2 * D], F32)
    scrows = sbuf("scrows", [48, 3 * 128], F32)
    s0tmp = sbuf("s0tmp", [128, 32], F32)
    ident, modsb, vecs, sclT = T["identsb"], T["modsb"], T["vecs"], T["sclT"]

    p.dma("sync", ident[:], T["ident"][:], sem="d_const")
    p.op("vector", "memset", T["onesr"][:], 1.0)
    p.op("vector", "memset", T["onesf"][:], 1.0)
    p.op("vector", "memset", T["consts"][:, 0:1], EPS, inc=("dve", 1))
    p.dma("sync", c16[:], T["c"].rearrange("a (g f) -> (a g) f", f=128), sem="d_const")
    p.dma("sync", scrows[:, 0:128], T["gsc"].rearrange("a (g f) -> (a g) f", f=128), sem="d_const")
    p.dma("sync", scrows[:, 128:256], T["usc"].rearrange("a (g f) -> (a g) f", f=128), sem="d_const")
    p.dma("sync", scrows[0:16, 256:384], T["dsc"].rearrange("a (g f) -> (a g) f", f=128), sem="d_const")
    d_all = p.now("d_const")

    p.wait("scalar", "d_const", d_all)
    p.op("scalar", "activation", c16[:], c16[:], Act.Silu, inc=("act", 1))
    p.wait("tensor", "act")
    p.wait("tensor", "d_const", d_all)
    p.op("tensor", "transpose", T["ps0"][:, 0:16], c16[:], ident[0:16, 0:16],
         inc=("pe", 1))
    p.op("tensor", "transpose", T["ps1"][:, 0:48], scrows[:, 0:128],
         ident[0:48, 0:48], inc=("pe", 1))
    p.op("tensor", "transpose", T["ps2"][:, 0:48], scrows[:, 128:256],
         ident[0:48, 0:48], inc=("pe", 1))
    p.op("tensor", "transpose", T["ps3"][:, 0:16], scrows[0:16, 256:384],
         ident[0:16, 0:16], inc=("pe", 1))
    p.wait("scalar", "pe")
    p.op("scalar", "copy", siluc[:], T["ps0"][:, 0:16])
    p.op("scalar", "copy", sclT[:, 0:FCI], T["ps1"][:, 0:48])
    p.op("scalar", "copy", sclT[:, FCI:2 * FCI], T["ps2"][:, 0:48])
    p.op("scalar", "copy", sclT[:, 2 * FCI:2 * FCI + FC], T["ps3"][:, 0:16],
         inc=("act", 1))

    act_after_silu = p.now("act")
    evac_marks = {}
    for jc in range(96):
        par = jc % 2
        buf = wmt[:, par * D:(par + 1) * D]
        buf3 = f3(buf, 16, 128)
        src = fmaj(T["wmod"][:, jc * 128:(jc + 1) * 128])
        if jc >= 2:
            p.wait("sync", "act", evac_marks[jc - 2])
        p.dma("sync", buf3, src, sem=f"d_wm{par}")
        dmark = p.now(f"d_wm{par}")
        pst = T[f"ps{4 + par}"][:, 0:1]
        p.wait("tensor", f"d_wm{par}", dmark)
        if jc == 0:
            p.wait("tensor", "act", act_after_silu)
        if jc >= 2:
            p.wait("tensor", "act", evac_marks[jc - 2])
        for kc in range(16):
            p.op("tensor", "matmul", pst, buf[:, kc * 128:(kc + 1) * 128],
                 T["siluc"][:, kc:kc + 1], start=(kc == 0), stop=(kc == 15),
                 inc=(("pe", 1) if kc == 15 else ()))
        p.wait("scalar", "pe")
        evac_marks[jc] = p.op("scalar", "copy", modsb[:, jc:jc + 1], pst,
                              inc=("act", 1))

    sa, ca, ga = modsb[:, 0:16], modsb[:, 16:32], modsb[:, 32:48]
    sb_, cb, gb = modsb[:, 48:64], modsb[:, 64:80], modsb[:, 80:96]
    ca1, sav = vecs[:, 0:16], vecs[:, 16:32]
    cb1, sbv = vecs[:, 32:48], vecs[:, 48:64]
    gav, gbdsc = vecs[:, 64:80], vecs[:, 80:96]
    t0, t1 = s0tmp[:, 0:16], s0tmp[:, 16:32]
    p.wait("vector", "act")
    v = lambda m, *a, **k: p.op("vector", m, *a, **k)
    v("tensor_scalar", ca1, ca, 1.0, None, Alu.add)
    v("tensor_copy", sav, sa)
    v("tensor_scalar", cb1, cb, 1.0, None, Alu.add)
    v("tensor_copy", sbv, sb_)
    def vs(m, *a):
        p.op("vector", m, *a, inc=("dve", 1))
        p.wait("vector", "dve")
    for g_in, g_out, mul_dsc in ((ga, gav, False), (gb, gbdsc, True)):
        vs("scalar_tensor_tensor", t0, g_in, -1.0, g_in, Alu.mult, Alu.max)
        vs("tensor_single_scalar", t0, t0, 0.25, Alu.max)
        vs("tensor_scalar", t1, g_in, 0.0, 2.0, Alu.is_ge, Alu.mult)
        vs("tensor_mul", t1, t1, t0)
        vs("tensor_sub", g_out, t1, t0)
        if mul_dsc:
            vs("tensor_mul", g_out, g_out, sclT[:, 2 * FCI:2 * FCI + FC])
    grow = sbuf("s0_grow", [16, 128], F32)
    growr = sbuf("s0_growr", [1, 2 * D], F32)
    p.wait("tensor", "dve")
    p.op("tensor", "transpose", T["ps0"][0:16, 0:128], gbdsc, ident[:],
         inc=("pe", 1))
    p.wait("scalar", "pe")
    p.op("scalar", "copy", grow[:], T["ps0"][0:16, 0:128], inc=("act", 1))
    p.wait("gpsimd", "act")
    p.dma("gpsimd", T["gtmp"][:], grow[:], sem="d_gb")
    p.wait("gpsimd", "d_gb")
    p.dma("gpsimd", growr[0:1, 0:D],
          T["gtmp"].rearrange("g f -> (g f)").unsqueeze(0), sem="d_gb")
    p.wait("tensor", "d_gb")
    for jj in range(4):
        if jj > 0:
            p.wait("tensor", "act")
        p.op("tensor", "matmul", T["ps1"][:, 0:512], T["onesr"][0:1, 0:128],
             growr[0:1, jj * 512:(jj + 1) * 512], start=True, stop=True,
             inc=("pe", 1))
        p.wait("scalar", "pe")
        p.op("scalar", "copy", T["gbrep"][:, jj * 512:(jj + 1) * 512],
             T["ps1"][:, 0:512], inc=("act", 1))
    p.barrier()


def build_s1(p, T, sbuf):
    xin = sbuf("s1_xin", [128, 4 * D], F32)
    xfsb = sbuf("s1_xf", [128, FC * CH], F32)
    sq = sbuf("s1_sq", [128, 2 * CH], F32)
    mrep = sbuf("s1_mrep", [128, 2 * CH], F32)
    rrep = sbuf("s1_rrep", [128, 2 * CH], F32)
    vtmp = sbuf("s1_vtmp", [128, CH], F32)
    h1f = sbuf("s1_h1f", [128, CH], F32)
    h1st = sbuf("s1_h1st", [128, 2 * FC * CH], F32)
    ident, ones, vecs = T["identsb"], T["onesf"], T["vecs"]

    evac_done = {}
    stat2_done = {}
    apply_done = {}
    wr_marks = {}

    for ci in range(NCH):
        par = ci % 2
        xc = lambda g: xin[:, g * D:(g + 1) * D]
        if ci >= 1:
            p.wait("sync", "pe", stat2_done[ci * FC - 1])
        for g in range(4):
            p.dma("sync", xc(g), T["x"][(ci * 4 + g) * 128:(ci * 4 + g + 1) * 128, :])
        xload = p.now("d_sync")

        psS1 = T[f"ps{2 + 2 * par}"]
        psS2 = T[f"ps{3 + 2 * par}"]

        for j in range(FC):
            gi = ci * FC + j
            trb = T[f"ps{gi % 2}"]
            xft = xfsb[:, j * CH:(j + 1) * CH]
            sqt = sq[:, (j % 2) * CH:(j % 2 + 1) * CH]
            if j == 0:
                p.wait("tensor", "d_sync", xload)
            if gi - 2 in evac_done:
                p.wait("tensor", "act", evac_done[gi - 2])
            for g in range(4):
                p.op("tensor", "transpose", trb[:, g * 128:(g + 1) * 128],
                     xc(g)[:, j * 128:(j + 1) * 128], ident[:],
                     inc=(("pe", 1) if g == 3 else ()))
            tr_pe = p.now("pe")
            p.wait("scalar", "pe", tr_pe)
            if (ci - 1, j) in apply_done:
                p.wait("scalar", "dve", apply_done[(ci - 1, j)])
            if j == 0 and ci - 1 in wr_marks:
                p.wait("scalar", f"d_wr{(ci - 1) % 2}", wr_marks[ci - 1])
            if gi - 2 in stat2_done:
                p.wait("scalar", "pe", stat2_done[gi - 2])
            p.op("scalar", "copy", xft, trb[:, 0:CH])
            evac_done[gi] = p.op("scalar", "activation", sqt, xft, Act.Square,
                                 inc=("act", 1))
            p.wait("tensor", "act", evac_done[gi])
            if ci >= 2 and j == 0:
                p.wait("tensor", "dve", apply_done[(ci - 2, FC - 1)])
            p.op("tensor", "matmul", psS1[:], ones[:], xft,
                 start=(j == 0), stop=(j == FC - 1))
            stat2_done[gi] = p.op("tensor", "matmul", psS2[:], ones[:], sqt,
                                  start=(j == 0), stop=(j == FC - 1),
                                  inc=("pe", 1))

        mr = mrep[:, par * CH:(par + 1) * CH]
        rr = rrep[:, par * CH:(par + 1) * CH]
        p.wait("scalar", "pe", stat2_done[ci * FC + FC - 1])
        if ci >= 2:
            p.wait("scalar", "dve", apply_done[(ci - 2, FC - 1)])
        p.op("scalar", "activation", mr, psS1[:], Act.Copy, scale=1.0 / D,
             inc=("act", 1))
        mr_act = p.now("act")
        p.wait("vector", "act", mr_act)
        p.op("vector", "tensor_mul", vtmp[:], mr, mr)
        p.op("vector", "scalar_tensor_tensor", rr, psS2[:], 1.0 / D, vtmp[:],
             Alu.mult, Alu.subtract, inc=("dve", 1))
        p.wait("scalar", "dve")
        p.op("scalar", "activation", rr, rr, Act.Sqrt,
             bias=T["consts"][:, 0:1], inc=("act", 1))
        p.wait("vector", "act")
        p.op("vector", "reciprocal", rr, rr, inc=("dve", 1))

        for j in range(FC):
            xft = xfsb[:, j * CH:(j + 1) * CH]
            hst = h1st[:, (par * FC + j) * CH:(par * FC + j + 1) * CH]
            if j == 0 and ci - 2 in wr_marks:
                p.wait("vector", f"d_wr{ci % 2}", wr_marks[ci - 2])
            p.op("vector", "tensor_sub", h1f[:], xft, mr)
            p.op("vector", "tensor_mul", h1f[:], h1f[:], rr)
            p.op("vector", "tensor_scalar", hst, h1f[:],
                 vecs[:, 0 + j:1 + j], vecs[:, 16 + j:17 + j],
                 Alu.mult, Alu.add, inc=("dve", 1))
            apply_done[(ci, j)] = p.now("dve")

        p.wait("gpsimd", "act", evac_done[ci * FC + FC - 1])
        p.dma("gpsimd", fmaj(T["xf"][:, ci * CH:(ci + 1) * CH]),
              f3(xfsb[:], FC, CH), sem=f"d_wr{ci % 2}")
        p.wait("gpsimd", "dve", apply_done[(ci, FC - 1)])
        wr_marks[ci] = p.dma("gpsimd", fmaj(T["h1f"][:, ci * CH:(ci + 1) * CH]),
              f3(h1st[:, par * FC * CH:(par + 1) * FC * CH], FC, CH),
              sem=f"d_wr{ci % 2}")

    p.barrier()


def build_s2(p, T, sbuf):
    QW = 512
    wst = sbuf("s2_wst", [128, 4 * D], F32)
    wt = {w: sbuf(f"s2_w{w}", [128, FC * QW], F32) for w in ("q", "k", "v")}
    h1t = sbuf("s2_h1", [128, FC * CH], F32)
    stg = sbuf("s2_stg", [128, 4 * CH], F32)
    br = sbuf("biasrows", [1, 3 * D], F32)
    onesr = T["onesr"]
    p.dma("sync", br[0:1, 0:D], T["bq"][:])
    p.dma("sync", br[0:1, D:2 * D], T["bk"][:])
    p.dma("sync", br[0:1, 2 * D:3 * D], T["bv"][:])

    tr_evac = {}
    acc_marks = {}
    stg_dma = 0
    h1_use = 0
    wst_free = 0
    gbank = 0
    WD = {"q": T["wq"], "k": T["wk"], "v": T["wv"]}
    OUT = {"q": T["qf"], "k": T["kf"], "v": T["vf"]}

    for qi in range(4):
        for wi, w in enumerate(("q", "k", "v")):
            if wst_free:
                p.wait("sync", "pe", wst_free)
            p.dma("sync", f3(wst[:], 4, D),
                  WD[w][qi * QW:(qi + 1) * QW, :].rearrange(
                      "(cc pp) k -> cc pp k", pp=128).transpose([1, 0, 2]))
            wld = p.now("d_sync")
            for cc in range(4):
                for kc in range(FC):
                    gt = (qi * 3 + wi) * 64 + cc * FC + kc
                    trb = T[f"ps{6 + gt % 2}"]
                    if cc == 0 and kc == 0:
                        p.wait("tensor", "d_sync", wld)
                    if gt - 2 in tr_evac:
                        p.wait("tensor", "act", tr_evac[gt - 2])
                    p.op("tensor", "transpose", trb[:, 0:128],
                         wst[:, cc * D + kc * 128:cc * D + (kc + 1) * 128],
                         T["identsb"][:], inc=("pe", 1))
                    tp = p.now("pe")
                    p.wait("scalar", "pe", tp)
                    tr_evac[gt] = p.op(
                        "scalar", "copy",
                        wt[w][:, kc * QW + cc * 128:kc * QW + (cc + 1) * 128],
                        trb[:, 0:128], inc=("act", 1))
            wst_free = p.now("pe")

        for tc in range(NCH):
            if h1_use:
                p.wait("sync", "pe", h1_use)
            p.dma("sync", f3(h1t[:], FC, CH), fmaj(T["h1f"][:, tc * CH:(tc + 1) * CH]))
            h1ld = p.now("d_sync")
            p.wait("tensor", "d_sync", h1ld)
            if tc == 0:
                p.wait("tensor", "act", tr_evac[(qi * 3 + 2) * 64 + 63])

            for w in ("q", "k", "v"):
                for cc in range(4):
                    bank = T[f"ps{gbank % 6}"]
                    gbank += 1
                    if (gbank - 1) % 6 in acc_marks and gbank > 6:
                        p.wait("tensor", "act", acc_marks[(gbank - 1) % 6])
                    if w in ("q", "k"):
                        boff = (0 if w == "q" else D) + qi * QW + cc * 128
                        p.op("tensor", "matmul", bank[:], br[0:1, boff:boff + 128],
                             onesr[0:1, 0:CH], start=True, stop=False)
                        for kc in range(FC):
                            p.op("tensor", "matmul", bank[:],
                                 wt[w][:, kc * QW + cc * 128:kc * QW + (cc + 1) * 128],
                                 h1t[:, kc * CH:(kc + 1) * CH], start=False,
                                 stop=(kc == FC - 1),
                                 inc=(("pe", 1) if kc == FC - 1 else ()))
                    else:
                        boff = 2 * D + qi * QW
                        p.op("tensor", "matmul", bank[:], onesr[0:1, 0:128],
                             br[0:1, boff:boff + QW], start=True, stop=False)
                        for kc in range(FC):
                            p.op("tensor", "matmul", bank[:],
                                 h1t[:, kc * CH + cc * 128:kc * CH + (cc + 1) * 128],
                                 wt[w][:, kc * QW:(kc + 1) * QW], start=False,
                                 stop=(kc == FC - 1),
                                 inc=(("pe", 1) if kc == FC - 1 else ()))
                    mme = p.now("pe")
                    p.wait("scalar", "pe", mme)
                    if stg_dma:
                        p.wait("scalar", "d_gpsimd", stg_dma)
                    acc_marks[(gbank - 1) % 6] = p.op(
                        "scalar", "copy", stg[:, cc * CH:(cc + 1) * CH], bank[:],
                        inc=("act", 1))
                p.wait("gpsimd", "act", acc_marks[(gbank - 1) % 6])
                if w in ("q", "k"):
                    dd = OUT[w][qi * QW:(qi + 1) * QW, tc * CH:(tc + 1) * CH]
                    stg_dma = p.dma("gpsimd", fmaj(dd), f3(stg[:], 4, CH))
                else:
                    dd = OUT[w][tc * CH:(tc + 1) * CH, qi * QW:(qi + 1) * QW]
                    stg_dma = p.dma(
                        "gpsimd",
                        dd.rearrange("(ts pp) cc -> ts pp cc", pp=128).transpose([1, 0, 2]),
                        f3(stg[:], 4, CH))
            h1_use = p.now("pe")

    p.barrier()


S3_HEADS = None
S3_WINS = None
S3_DUMP = False


def build_s3(p, T, sbuf):
    heads = list(S3_HEADS if S3_HEADS is not None else range(NH))
    wins = list(S3_WINS if S3_WINS is not None else range(NW))
    qb = sbuf("s3_q", [128, 2 * S], F32)
    kb = sbuf("s3_k", [128, 2 * S], F32)
    vb = sbuf("s3_v", [128, 2 * S], F32)
    xfb = sbuf("s3_xf", [128, 2 * S], F32)
    x2row = sbuf("s3_x2", [128, 2 * S], F32)
    caussb = sbuf("s3_caus", [128, 2 * GS], F32)
    ssb = sbuf("s3_s", [128, 2 * GS], F32)
    nrm = sbuf("s3_nrm", [128, 2 * GS], F32)
    red = sbuf("s3_red", [128, 8], F32)
    aT = sbuf("s3_aT", [128, 2 * GS], F32)
    vecs = T["vecs"]

    p.dma("sync", caussb[:], T["caus"][:], sem="d_h0")
    ld_marks = {}
    hpar = {}
    head_done = {}
    st_marks = {}
    sc_ev = {}
    tr_ev = {}
    av_ev = {}
    load_order = []

    def load_head(h):
        par = len(load_order) % 2
        hpar[h] = par
        load_order.append(h)
        sl = slice(par * S, (par + 1) * S)
        if len(load_order) > 2:
            pe_m, dve_m = head_done[load_order[-3]]
            p.wait("sync", "pe", pe_m)
            p.wait("sync", "dve", dve_m)
        hs = f"d_h{par}"
        p.dma("sync", qb[:, sl], T["qf"][h * 128:(h + 1) * 128, :], sem=hs)
        p.dma("sync", kb[:, sl], T["kf"][h * 128:(h + 1) * 128, :], sem=hs)
        p.dma("sync", f3(vb[:, sl], 32, 128),
              T["vf"][:, h * 128:(h + 1) * 128].rearrange(
                  "(ts pp) c -> ts pp c", pp=128).transpose([1, 0, 2]), sem=hs)
        p.dma("sync", xfb[:, sl], T["xf"][h * 128:(h + 1) * 128, :], sem=hs)
        ld_marks[h] = p.now(hs)

    for h0 in heads[:2]:
        load_head(h0)
    SC = float(1.0 / np.sqrt(128.0))

    for hi_, h in enumerate(heads):
        par = hpar[h]
        off = par * S
        shift = 128 if h >= 8 else 0
        p.wait("tensor", f"d_h{par}", ld_marks[h])
        gav = vecs[:, 64 + h:65 + h]

        for w in wins:
            b = w * GS + shift
            wrap = (b + 2 * GS > S) or (b + GS > S)
            kcol = [(b % S), ((b + 128) % S)]
            for qc in range(2):
                qo = (b + qc * 128) % S
                sb_i = (w * 2 + qc) % 4
                bank = T[f"ps{sb_i}"]
                if sb_i in sc_ev:
                    p.wait("tensor", "dve", sc_ev[sb_i])
                if not wrap:
                    p.op("tensor", "matmul", bank[:, 0:GS],
                         qb[:, off + qo:off + qo + 128],
                         kb[:, off + b:off + b + GS],
                         start=True, stop=True, inc=("pe", 1))
                else:
                    for kci, ko in enumerate(kcol):
                        p.op("tensor", "matmul",
                             bank[:, kci * 128:(kci + 1) * 128],
                             qb[:, off + qo:off + qo + 128],
                             kb[:, off + ko:off + ko + 128],
                             start=True, stop=True,
                             inc=(("pe", 1) if kci == 1 else ()))
                mm = p.now("pe")
                sst = ssb[:, qc * GS:(qc + 1) * GS]
                p.wait("vector", "pe", mm)
                p.op("vector", "scalar_tensor_tensor", sst, bank[:, 0:GS], SC,
                     caussb[:, qc * GS:(qc + 1) * GS], Alu.mult, Alu.add)
                nm = red[:, qc * 4:qc * 4 + 1]
                se = red[:, qc * 4 + 1:qc * 4 + 2]
                ri = red[:, qc * 4 + 2:qc * 4 + 3]
                p.op("vector", "reduce_max", nm, sst, AX, negate=True,
                     inc=("dve", 1))
                sc_ev[sb_i] = p.now("dve")
                p.wait("scalar", "dve")
                p.op("scalar", "activation", sst, sst, Act.Exp, bias=nm,
                     accum_out=se, inc=("act", 1))
                p.wait("vector", "act")
                p.op("vector", "reciprocal", ri, se, inc=("dve", 1))
                p.wait("vector", "dve")
                p.op("vector", "tensor_scalar", nrm[:, qc * GS:(qc + 1) * GS],
                     sst, ri, None, Alu.mult, inc=("dve", 1))
                nrm_dve = p.now("dve")
                for kci in range(2):
                    tb_i = 4 + kci
                    tbank = T[f"ps{tb_i}"]
                    if tb_i in tr_ev:
                        p.wait("tensor", "act", tr_ev[tb_i])
                    p.wait("tensor", "dve", nrm_dve)
                    p.op("tensor", "transpose", tbank[:, 0:128],
                         nrm[:, qc * GS + kci * 128:qc * GS + (kci + 1) * 128],
                         T["identsb"][:], inc=("pe", 1))
                    tp = p.now("pe")
                    p.wait("scalar", "pe", tp)
                    tr_ev[tb_i] = p.op(
                        "scalar", "copy",
                        aT[:, kci * GS + qc * 128:kci * GS + (qc + 1) * 128],
                        tbank[:, 0:128], inc=("act", 1))
            ab_i = 6 + w % 2
            abank = T[f"ps{ab_i}"]
            if ab_i in av_ev:
                p.wait("tensor", "dve", av_ev[ab_i])
            p.wait("tensor", "act", tr_ev[5])
            for kci, ko in enumerate(kcol):
                ts = ko // 128
                p.op("tensor", "matmul", abank[:, 0:GS],
                     vb[:, off + ts * 128:off + (ts + 1) * 128],
                     aT[:, kci * GS:(kci + 1) * GS],
                     start=(kci == 0), stop=(kci == 1),
                     inc=(("pe", 1) if kci == 1 else ()))
            avm = p.now("pe")
            p.wait("vector", "pe", avm)
            for piece in range(2):
                co = (b + piece * 128) % S
                p.op("vector", "scalar_tensor_tensor",
                     x2row[:, off + co:off + co + 128],
                     abank[:, piece * 128:(piece + 1) * 128], gav,
                     xfb[:, off + co:off + co + 128], Alu.mult, Alu.add,
                     inc=(("dve", 1) if piece == 1 else ()))
            av_ev[ab_i] = p.now("dve")

        p.wait("gpsimd", "dve", av_ev[6 + wins[-1] % 2])
        st_marks[h] = p.dma("gpsimd", T["x2"][h * 128:(h + 1) * 128, :],
                            x2row[:, off:off + S])
        head_done[h] = (p.now("pe"), p.now("dve"))
        if hi_ + 2 < len(heads):
            p.wait("vector", "d_gpsimd", st_marks[h])
            load_head(heads[hi_ + 2])

    p.barrier()


def build_s4(p, T, sbuf):
    x2sb = sbuf("s4_x2", [128, FC * CH], F32)
    sq = sbuf("s4_sq", [128, 2 * CH], F32)
    mrep = sbuf("s4_mrep", [128, 2 * CH], F32)
    rrep = sbuf("s4_rrep", [128, 2 * CH], F32)
    vtmp = sbuf("s4_vtmp", [128, CH], F32)
    h2f = sbuf("s4_h2f", [128, CH], F32)
    h2hi = sbuf("s4_h2hi", [128, 2 * FC * CH], BF16)
    h2lo = sbuf("s4_h2lo", [128, 2 * FC * CH], BF16)
    ones, vecs = T["onesf"], T["vecs"]

    stat_done = {}
    apply_done = {}
    wr_marks = {}
    sq_done = {}

    for ci in range(NCH):
        par = ci % 2
        if ci >= 1:
            p.wait("sync", "pe", stat_done[ci - 1])
            p.wait("sync", "dve", apply_done[(ci - 1, FC - 1)])
        p.dma("sync", f3(x2sb[:], FC, CH), fmaj(T["x2"][:, ci * CH:(ci + 1) * CH]))
        xload = p.now("d_sync")
        psS1 = T[f"ps{2 * par}"]
        psS2 = T[f"ps{2 * par + 1}"]
        for j in range(FC):
            gi = ci * FC + j
            xft = x2sb[:, j * CH:(j + 1) * CH]
            sqt = sq[:, (j % 2) * CH:(j % 2 + 1) * CH]
            if j == 0:
                p.wait("scalar", "d_sync", xload)
            if gi - 2 in sq_done:
                p.wait("scalar", "pe", sq_done[gi - 2])
            sqm = p.op("scalar", "activation", sqt, xft, Act.Square, inc=("act", 1))
            p.wait("tensor", "act", sqm)
            if ci >= 2 and j == 0:
                p.wait("tensor", "dve", apply_done[(ci - 2, FC - 1)])
            if j == 0:
                p.wait("tensor", "d_sync", xload)
            p.op("tensor", "matmul", psS1[:], ones[:], xft,
                 start=(j == 0), stop=(j == FC - 1))
            sq_done[gi] = p.op("tensor", "matmul", psS2[:], ones[:], sqt,
                               start=(j == 0), stop=(j == FC - 1), inc=("pe", 1))
        stat_done[ci] = sq_done[ci * FC + FC - 1]

        mr = mrep[:, par * CH:(par + 1) * CH]
        rr = rrep[:, par * CH:(par + 1) * CH]
        p.wait("scalar", "pe", stat_done[ci])
        if ci >= 2:
            p.wait("scalar", "dve", apply_done[(ci - 2, FC - 1)])
        p.op("scalar", "activation", mr, psS1[:], Act.Copy, scale=1.0 / D,
             inc=("act", 1))
        p.wait("vector", "act")
        p.op("vector", "tensor_mul", vtmp[:], mr, mr)
        p.op("vector", "scalar_tensor_tensor", rr, psS2[:], 1.0 / D, vtmp[:],
             Alu.mult, Alu.subtract, inc=("dve", 1))
        p.wait("scalar", "dve")
        p.op("scalar", "activation", rr, rr, Act.Sqrt, bias=T["consts"][:, 0:1],
             inc=("act", 1))
        p.wait("vector", "act")
        p.op("vector", "reciprocal", rr, rr, inc=("dve", 1))

        for j in range(FC):
            xft = x2sb[:, j * CH:(j + 1) * CH]
            hhi = h2hi[:, (par * FC + j) * CH:(par * FC + j + 1) * CH]
            hlo = h2lo[:, (par * FC + j) * CH:(par * FC + j + 1) * CH]
            if j == 0 and ci - 2 in wr_marks:
                p.wait("vector", f"d_w4{ci % 2}", wr_marks[ci - 2])
            p.op("vector", "tensor_sub", h2f[:], xft, mr)
            p.op("vector", "tensor_mul", h2f[:], h2f[:], rr)
            p.op("vector", "tensor_scalar", h2f[:], h2f[:],
                 vecs[:, 32 + j:33 + j], vecs[:, 48 + j:49 + j], Alu.mult, Alu.add)
            p.op("vector", "tensor_copy", hhi, h2f[:])
            p.op("vector", "tensor_tensor", hlo, h2f[:], hhi, Alu.subtract,
                 inc=("dve", 1))
            apply_done[(ci, j)] = p.now("dve")

        p.wait("gpsimd", "dve", apply_done[(ci, FC - 1)])
        p.dma("gpsimd", fmaj(T["h2h"][:, ci * CH:(ci + 1) * CH]),
              f3(h2hi[:, par * FC * CH:(par + 1) * FC * CH], FC, CH),
              sem=f"d_w4{ci % 2}")
        wr_marks[ci] = p.dma("gpsimd", fmaj(T["h2l"][:, ci * CH:(ci + 1) * CH]),
                             f3(h2lo[:, par * FC * CH:(par + 1) * FC * CH], FC, CH),
                             sem=f"d_w4{ci % 2}")

    p.barrier()


PW = 1536
NP = DI // PW
LCH = 768


def build_s5(p, T, sbuf):
    gq = sbuf("s5_gq", [128, FC * PW], BF16)
    uq = sbuf("s5_uq", [128, FC * PW], BF16)
    f8st = sbuf("s5_f8", [128, FC * PW], F8)
    h2ht = sbuf("s5_h2h", [128, FC * CH], BF16)
    h2lt = sbuf("s5_h2l", [128, FC * CH], BF16)
    gf = sbuf("s5_gf", [128, CH], F32)
    ff = sbuf("s5_ff", [128, CH], F32)
    fsh = sbuf("s5_fsh", [128, 12 * CH], BF16)
    fsl = sbuf("s5_fsl", [128, 12 * CH], BF16)
    ltmp = {k: sbuf("s5_" + k, [128, LCH], BF16) for k in LADDER_TMPS}
    sclT = T["sclT"]

    lad_done = {}
    acc_ev = {}
    st_marks = {}
    h2_use = 0
    gbank = 0

    for pi in range(NP):
        for wi, (wn, dst) in enumerate((("gw", gq), ("uw", uq))):
            if pi > 0:
                p.wait("gpsimd", "pe", h2_use)
            if lad_done:
                p.wait("gpsimd", "dve", max(lad_done.values()))
            p.dma("gpsimd", f3(f8st[:], FC, PW),
                  T[wn][:, pi * PW:(pi + 1) * PW].rearrange(
                      "(kc pp) c -> kc pp c", pp=128).transpose([1, 0, 2]))
            wld = p.now("d_gpsimd")
            for kc in range(FC):
                if kc == 0:
                    p.wait("vector", "d_gpsimd", wld)
                ksl = slice(kc * PW, (kc + 1) * PW)
                p.op("vector", "tensor_copy", dst[:, ksl], f8st[:, ksl])
                for sub in range(PW // LCH):
                    sl = slice(kc * PW + sub * LCH, kc * PW + (sub + 1) * LCH)
                    emit_e4m3(p, "vector", dst[:, sl], dst[:, sl], ltmp)
                p.op("vector", "tensor_copy", ltmp["lt"][:, 0:1],
                     dst[:, kc * PW:kc * PW + 1], inc=("dve", 1))
                lad_done[(pi, wi, kc)] = p.now("dve")
        q_ready = lad_done[(pi, 1, FC - 1)]

        for tc in range(NCH):
            if h2_use:
                p.wait("sync", "pe", h2_use)
            p.dma("sync", f3(h2ht[:], FC, CH), fmaj(T["h2h"][:, tc * CH:(tc + 1) * CH]))
            p.dma("sync", f3(h2lt[:], FC, CH), fmaj(T["h2l"][:, tc * CH:(tc + 1) * CH]))
            hld = p.now("d_sync")
            p.wait("tensor", "d_sync", hld)
            if tc == 0:
                p.wait("tensor", "dve", q_ready)
            for cc in range(PW // 128):
                gcol = pi * (PW // 128) + cc
                bg = T[f"ps{2 * (gbank % 2)}"]
                bu = T[f"ps{2 * (gbank % 2) + 1}"]
                gbank += 1
                if 2 * ((gbank - 1) % 2) in acc_ev and gbank > 2:
                    p.wait("tensor", "dve", acc_ev[2 * ((gbank - 1) % 2)])
                for wq_t, bank in ((gq, bg), (uq, bu)):
                    for kc in range(FC):
                        wt_ = wq_t[:, kc * PW + cc * 128:kc * PW + (cc + 1) * 128]
                        p.op("tensor", "matmul", bank[:], wt_,
                             h2ht[:, kc * CH:(kc + 1) * CH], start=(kc == 0), stop=False)
                        p.op("tensor", "matmul", bank[:], wt_,
                             h2lt[:, kc * CH:(kc + 1) * CH], start=False,
                             stop=(kc == FC - 1),
                             inc=(("pe", 1) if kc == FC - 1 else ()))
                mme = p.now("pe")
                p.wait("scalar", "pe", mme)
                p.op("scalar", "activation", gf[:], bg[:], Act.Silu,
                     scale=sclT[:, gcol:gcol + 1], inc=("act", 1))
                p.wait("vector", "act")
                if cc == 0 and st_marks:
                    p.wait("vector", "d_gpsimd", max(st_marks.values()))
                p.op("vector", "scalar_tensor_tensor", ff[:], bu[:],
                     sclT[:, FCI + gcol:FCI + gcol + 1], gf[:], Alu.mult, Alu.mult)
                p.op("vector", "tensor_copy", fsh[:, cc * CH:(cc + 1) * CH], ff[:])
                p.op("vector", "tensor_tensor", fsl[:, cc * CH:(cc + 1) * CH],
                     ff[:], fsh[:, cc * CH:(cc + 1) * CH], Alu.subtract,
                     inc=("dve", 1))
                acc_ev[2 * ((gbank - 1) % 2)] = p.now("dve")
            p.wait("gpsimd", "dve", acc_ev[2 * ((gbank - 1) % 2)])
            dh = T["fsh"][pi * PW:(pi + 1) * PW, tc * CH:(tc + 1) * CH]
            dl = T["fsl"][pi * PW:(pi + 1) * PW, tc * CH:(tc + 1) * CH]
            p.dma("gpsimd", fmaj(dh), f3(fsh[:], 12, CH))
            st_marks[(pi, tc)] = p.dma("gpsimd", fmaj(dl), f3(fsl[:], 12, CH))
            h2_use = p.now("pe")

    p.barrier()


C6 = 256
N6 = S // C6


def build_s6(p, T, sbuf):
    HW_ = 1024
    C6 = 128
    N6 = S // C6
    dwq = sbuf("s6_dwq", [128, FCI * HW_], BF16)
    f8st = sbuf("s6_f8", [128, FCI * (HW_ // 2)], F8)
    fsht = sbuf("s6_fsh", [128, 2 * FCI * C6], BF16)
    fslt = sbuf("s6_fsl", [128, 2 * FCI * C6], BF16)
    x2sb = sbuf("s6_x2", [128, FC * C6], F32)
    x2t = sbuf("s6_x2t", [128, D], F32)
    ost = sbuf("s6_ost", [128, HW_], F32)
    ltmp = {k: sbuf("s6_" + k, [128, 256], BF16) for k in LADDER_TMPS}
    gbrep = T["gbrep"]
    ident = T["identsb"]

    fs_use = 0
    lad_done = 0
    acc_ev = {}
    tr_ev = {}
    st_marks = {}
    fld_marks = {}
    x2tr_done = {}

    for half in range(2):
        for sub in range(2):
            if half > 0 or sub > 0:
                p.wait("gpsimd", "pe", fs_use)
                p.wait("gpsimd", "dve", lad_done)
            p.dma("gpsimd", f3(f8st[:], FCI, HW_ // 2),
                  T["dw"][:, half * HW_ + sub * (HW_ // 2):
                          half * HW_ + (sub + 1) * (HW_ // 2)].rearrange(
                      "(kc pp) c -> kc pp c", pp=128).transpose([1, 0, 2]))
            wld = p.now("d_gpsimd")
            p.wait("vector", "d_gpsimd", wld)
            for kc in range(FCI):
                ssl = slice(kc * (HW_ // 2), (kc + 1) * (HW_ // 2))
                dsl = slice(kc * HW_ + sub * (HW_ // 2),
                            kc * HW_ + (sub + 1) * (HW_ // 2))
                p.op("vector", "tensor_copy", dwq[:, dsl], f8st[:, ssl])
                for q4 in range(2):
                    qsl = slice(dsl.start + q4 * 256, dsl.start + (q4 + 1) * 256)
                    emit_e4m3(p, "vector", dwq[:, qsl], dwq[:, qsl], ltmp)
            p.op("vector", "tensor_copy", ltmp["lt"][:, 0:1], dwq[:, 0:1],
                 inc=("dve", 1))
            lad_done = p.now("dve")

        for tc in range(N6):
            tpar = tc % 2
            if tc >= 2 or half > 0:
                prev = fld_marks.get((half, tc - 2), fld_marks.get((half - 1, N6 - 2 + tc), 0))
                p.wait("sync", "pe", fs_use)
            p.dma("sync", f3(fsht[:, tpar * FCI * C6:(tpar + 1) * FCI * C6], FCI, C6),
                  fmaj(T["fsh"][:, tc * C6:(tc + 1) * C6]))
            fld_marks[(half, tc)] = p.dma(
                "sync", f3(fslt[:, tpar * FCI * C6:(tpar + 1) * FCI * C6], FCI, C6),
                fmaj(T["fsl"][:, tc * C6:(tc + 1) * C6]))
            fld = p.now("d_sync")
            if x2tr_done:
                p.wait("scalar", "pe", max(x2tr_done.values()))
            p.dma("scalar", f3(x2sb[:], FC, C6), fmaj(T["x2"][:, tc * C6:(tc + 1) * C6]))
            x2ld = p.now("d_scalar")
            p.wait("tensor", "d_scalar", x2ld)
            for jf in range(FC):
                tb = T[f"ps{4 + jf % 2}"]
                if (4 + jf % 2) in tr_ev:
                    p.wait("tensor", "act", tr_ev[4 + jf % 2])
                p.op("tensor", "transpose", tb[:, 0:128],
                     x2sb[:, jf * C6:(jf + 1) * C6], ident[:], inc=("pe", 1))
                tp = p.now("pe")
                p.wait("scalar", "pe", tp)
                if jf < 2 and acc_ev:
                    p.wait("scalar", "dve", max(acc_ev.values()))
                tr_ev[4 + jf % 2] = p.op("scalar", "copy",
                                         x2t[:, jf * 128:(jf + 1) * 128],
                                         tb[:, 0:128], inc=("act", 1))
            x2tr_done[tc % 2] = p.now("pe")
            x2t_act = p.now("act")

            p.wait("tensor", "d_sync", fld)
            if tc == 0:
                p.wait("tensor", "dve", lad_done)
            for cg in range(2):
                bank = T[f"ps{cg}"]
                if cg in acc_ev and (tc > 0 or half > 0):
                    p.wait("tensor", "dve", acc_ev[cg])
                for kc in range(FCI):
                    fh = fsht[:, tpar * FCI * C6 + kc * C6:
                              tpar * FCI * C6 + (kc + 1) * C6]
                    fl = fslt[:, tpar * FCI * C6 + kc * C6:
                              tpar * FCI * C6 + (kc + 1) * C6]
                    wslice = dwq[:, kc * HW_ + cg * 512:kc * HW_ + (cg + 1) * 512]
                    p.op("tensor", "matmul", bank[:], fh, wslice,
                         start=(kc == 0), stop=False)
                    p.op("tensor", "matmul", bank[:], fl, wslice,
                         start=False, stop=(kc == FCI - 1),
                         inc=(("pe", 1) if kc == FCI - 1 else ()))
                mme = p.now("pe")
                p.wait("vector", "pe", mme)
                p.wait("vector", "act", x2t_act)
                if st_marks:
                    p.wait("vector", "d_gpsimd", max(st_marks.values()))
                fo = half * HW_ + cg * 512
                p.op("vector", "tensor_mul", ost[:, cg * 512:(cg + 1) * 512],
                     bank[:], gbrep[:, fo:fo + 512])
                p.op("vector", "tensor_tensor", ost[:, cg * 512:(cg + 1) * 512],
                     ost[:, cg * 512:(cg + 1) * 512], x2t[:, fo:fo + 512],
                     Alu.add, inc=("dve", 1))
                acc_ev[cg] = p.now("dve")
            p.wait("gpsimd", "dve", acc_ev[1])
            st_marks[(half, tc)] = p.dma(
                "gpsimd", T["out"][tc * C6:(tc + 1) * C6, half * HW_:(half + 1) * HW_],
                ost[:])
            fs_use = p.now("pe")

    p.barrier(engines=("sync", "gpsimd"))



_CAUS = np.concatenate([
    np.where(np.arange(2 * GS // 2)[None, :] > np.arange(128)[:, None], -1e9, 0.0),
    np.where(np.arange(2 * GS // 2)[None, :] > 128 + np.arange(128)[:, None], -1e9, 0.0),
], axis=1).astype(np.float32)
_IDENT = np.eye(128, dtype=np.float32)


def kernel(L0_attn_x, L0_attn_Wq, L0_attn_bq, L0_attn_Wk, L0_attn_bk,
           L0_attn_Wv, L0_attn_bv, L0_ffn_gate_w, L0_ffn_up_w, L0_ffn_down_w,
           L0_ffn_gate_scale, L0_ffn_up_scale, L0_ffn_down_scale, L0_c, L0_wmod):
    from concourse.bass_utils import run_bass_kernel_spmd
    nc = build_nc(("s0", "s1", "s2", "s3", "s4", "s5", "s6"))
    com = dict(
        wq=np.ascontiguousarray(L0_attn_Wq), wk=np.ascontiguousarray(L0_attn_Wk),
        wv=np.ascontiguousarray(L0_attn_Wv),
        bq=np.ascontiguousarray(L0_attn_bq)[None],
        bk=np.ascontiguousarray(L0_attn_bk)[None],
        bv=np.ascontiguousarray(L0_attn_bv)[None],
        gw=np.ascontiguousarray(L0_ffn_gate_w), uw=np.ascontiguousarray(L0_ffn_up_w),
        dw=np.ascontiguousarray(L0_ffn_down_w),
        gsc=np.ascontiguousarray(L0_ffn_gate_scale)[None],
        usc=np.ascontiguousarray(L0_ffn_up_scale)[None],
        dsc=np.ascontiguousarray(L0_ffn_down_scale)[None],
        wmod=np.ascontiguousarray(L0_wmod),
        ident=_IDENT, caus=_CAUS,
    )
    in_maps = [dict(com, x=np.ascontiguousarray(L0_attn_x[i]),
                    c=np.ascontiguousarray(L0_c[i:i + 1])) for i in range(8)]
    res = run_bass_kernel_spmd(nc, in_maps, core_ids=list(range(8)))
    return np.stack([r["out"] for r in res.results], axis=0)


# version 06
# speedup vs baseline: 1.1050x; 1.1050x over previous
import numpy as np
import concourse.bass as bass
import concourse.mybir as mybir

F32 = mybir.dt.float32
BF16 = mybir.dt.bfloat16
F8 = mybir.dt.float8e4
Alu = mybir.AluOpType
Act = mybir.ActivationFunctionType
AX = mybir.AxisListType.X

ENGINES = ("sync", "vector", "scalar", "tensor", "gpsimd")


def fmaj(ap2d, ncols=None):
    return ap2d.rearrange("(j pp) t -> j pp t", pp=128).transpose([1, 0, 2])


def f3(tile_ap, n, c):
    return tile_ap.rearrange("p (j t) -> p j t", t=c)

D = 2048
S = 4096
FC = D // 128
NH = 16
HD = 128
GS = 256
NW = S // GS
DI = 6144
FCI = DI // 128
CH = 512
NCH = S // CH
EPS = 1e-5


class P:

    def __init__(self, nc):
        self.nc = nc
        self.q = {e: [] for e in ENGINES}
        self.sems = {}
        self.cnt = {}

    def sem(self, name):
        if name not in self.sems:
            self.sems[name] = self.nc.alloc_semaphore(name)
            self.cnt[name] = 0
        return self.sems[name]

    def op(self, eng, meth, *args, inc=(), **kw):
        if inc and isinstance(inc[0], str):
            inc = (inc,)
        handles = []
        for s, n in inc:
            self.sem(s)
            self.cnt[s] += n
            handles.append((self.sems[s], n))

        def thunk(e, meth=meth, args=args, kw=kw, handles=handles):
            ins = getattr(e, meth)(*args, **kw)
            for h, n in handles:
                ins = ins.then_inc(h, n)

        self.q[eng].append(thunk)
        return self.cnt[inc[0][0]] if inc else None

    def dma(self, eng, out, in_, sem=None, **kw):
        semname = sem or ("d_" + eng)
        return self.op(eng, "dma_start", inc=((semname, 16),), out=out, in_=in_, **kw)

    def wait(self, eng, semname, val=None):
        self.sem(semname)
        v = self.cnt[semname] if val is None else val
        if v <= 0:
            return
        h = self.sems[semname]
        self.q[eng].append(lambda e, h=h, v=v: e.wait_ge(h, v))

    def now(self, semname):
        self.sem(semname)
        return self.cnt[semname]

    def barrier(self, engines=ENGINES, producers=None):
        prods = producers if producers is not None else list(self.cnt)
        vals = [(s, self.cnt[s]) for s in prods]
        for e in engines:
            for s, v in vals:
                if v > 0:
                    self.wait(e, s, v)

    def flush(self):
        with self.nc.Block() as block:
            for e in ENGINES:
                thunks = self.q[e]
                if not thunks:
                    continue
                getattr(block, e)(lambda eng, thunks=thunks: [t(eng) for t in thunks])
            self.q = {e: [] for e in ENGINES}


LADDER_TMPS = ("ls", "lt", "lz", "la1", "la2", "la3", "la4", "lp", "lr", "lb0")


def emit_e4m3(p, eng, out, x, tmp):
    s, t, z, a1, a2, a3, a4, pp, r, b0 = (tmp[k] for k in LADDER_TMPS)
    o = lambda m, *a: p.op(eng, m, *a)
    o("tensor_single_scalar", t, x, -512.0, Alu.mult)
    o("scalar_tensor_tensor", s, x, 512.0, t, Alu.mult, Alu.max)
    o("tensor_single_scalar", a1, s, 16.0, Alu.is_ge)
    o("tensor_single_scalar", a2, s, 32.0, Alu.is_ge)
    o("tensor_single_scalar", a3, s, 64.0, Alu.is_ge)
    o("tensor_single_scalar", a4, s, 128.0, Alu.is_ge)
    o("tensor_copy", z, s)
    for a in (a1, a2, a3, a4):
        o("tensor_scalar", pp, a, -0.5, 1.0, Alu.mult, Alu.add)
        o("tensor_mul", z, z, pp)
    o("tensor_copy", r, z)
    for thr, w in ((15.5, 16.0), (7.5, 8.0), (3.5, 4.0), (1.5, 2.0)):
        o("tensor_scalar", t, r, thr, w, Alu.is_ge, Alu.mult)
        o("tensor_sub", r, r, t)
    o("tensor_scalar", b0, r, 0.5, 1.0, Alu.is_ge, Alu.mult)
    o("tensor_sub", r, r, b0)
    o("tensor_sub", r, z, r)
    o("tensor_sub", t, r, z)
    o("tensor_single_scalar", t, t, 0.5, Alu.is_equal)
    o("tensor_mul", t, t, b0)
    o("tensor_sub", r, r, t)
    for a in (a1, a2, a3, a4):
        o("tensor_scalar", pp, a, 1.0, None, Alu.add)
        o("tensor_mul", r, r, pp)
    o("tensor_single_scalar", r, r, 2.0 ** -9, Alu.mult)
    o("tensor_scalar", t, x, 0.0, 2.0, Alu.is_ge, Alu.mult)
    o("tensor_mul", t, t, r)
    o("tensor_sub", out, t, r)



class QStream:

    def __init__(self, p, T, f8st, oblk, ltmp):
        self.p, self.T = p, T
        self.f8st, self.oblk, self.ltmp = f8st, oblk, ltmp
        self.blocks = []
        for name, R, C in (("gw", D, DI), ("uw", D, DI), ("dw", DI, D)):
            for rb in range(R // 1024):
                for cb in range(C // 512):
                    self.blocks.append((name, R, C, rb, cb))
        self.bi = 0
        self.kri = 0
        self.load_mark = 0
        self.prev_store = 0
        self.prev_ladder = 0

    def done(self):
        return self.bi >= len(self.blocks)

    def step(self):
        if self.done():
            return
        p, T = self.p, self.T
        name, R, C, rb, cb = self.blocks[self.bi]
        if self.kri == 0:
            if self.prev_ladder:
                p.wait("gpsimd", "qdve", self.prev_ladder)
            src = T[name][rb * 1024:(rb + 1) * 1024,
                          cb * 512:(cb + 1) * 512].rearrange(
                "(kr pp) c -> kr pp c", pp=128).transpose([1, 0, 2])
            self.load_mark = p.dma("gpsimd", f3(self.f8st[:], 8, 512), src,
                                   sem="d_qi")
            p.wait("vector", "d_qi", self.load_mark)
            if self.prev_store:
                p.wait("vector", "d_qo", self.prev_store)
        kr = self.kri
        sl = slice(kr * 512, (kr + 1) * 512)
        p.op("vector", "tensor_copy", self.oblk[:, sl], self.f8st[:, sl])
        emit_e4m3(p, "vector", self.oblk[:, sl], self.oblk[:, sl], self.ltmp)
        p.op("vector", "tensor_copy", self.ltmp["lt"][:, 0:1],
             self.oblk[:, kr * 512:kr * 512 + 1], inc=("qdve", 1))
        self.kri += 1
        if self.kri == 8:
            self.prev_ladder = p.now("qdve")
            p.wait("gpsimd", "qdve", self.prev_ladder)
            dst = T[name + "q"][rb * 1024:(rb + 1) * 1024,
                                cb * 512:(cb + 1) * 512].rearrange(
                "(kr pp) c -> kr pp c", pp=128).transpose([1, 0, 2])
            self.prev_store = p.dma("gpsimd", dst, f3(self.oblk[:], 8, 512),
                                    sem="d_qo")
            self.kri = 0
            self.bi += 1

    def flush(self):
        while not self.done():
            self.step()


def build_nc(stages=("s0", "s1"), debug=()):
    nc = bass.Bass("TRN2", target_bir_lowering=False,
                   detect_race_conditions=False)
    T = {}

    def dram(name, shape, dtype, kind="Internal"):
        T[name] = nc.dram_tensor(name, shape, dtype, kind=kind).ap()
        return T[name]

    scope_guards = []

    def sbuf(name, shape, dtype):
        T[name] = nc.alloc_sbuf_tensor(name, shape, dtype).ap()
        return T[name]

    def sbuf_scoped(name, shape, dtype):
        g = nc.sbuf_tensor(name, shape, dtype)
        h = g.__enter__()
        scope_guards.append(g)
        T[name] = h.ap()
        return T[name]

    def free_scope():
        while scope_guards:
            scope_guards.pop().__exit__(None, None, None)

    def psum(name, shape):
        T[name] = nc.alloc_psum_tensor(name, shape, F32).ap()
        return T[name]

    dram("x", [S, D], F32, "ExternalInput")
    dram("c", [1, D], F32, "ExternalInput")
    dram("wmod", [D, 6 * D], F32, "ExternalInput")
    for w in ("wq", "wk", "wv"):
        dram(w, [D, D], F32, "ExternalInput")
    for b in ("bq", "bk", "bv"):
        dram(b, [1, D], F32, "ExternalInput")
    dram("gw", [D, DI], F32, "ExternalInput")
    dram("uw", [D, DI], F32, "ExternalInput")
    dram("dw", [DI, D], F32, "ExternalInput")
    dram("gsc", [1, DI], F32, "ExternalInput")
    dram("usc", [1, DI], F32, "ExternalInput")
    dram("dsc", [1, D], F32, "ExternalInput")
    dram("ident", [128, 128], F32, "ExternalInput")
    dram("caus", [128, 2 * GS], F32, "ExternalInput")
    dram("out", [S, D], F32, "ExternalOutput")

    def scratch(name, shape, dtype):
        dram(name, shape, dtype, "ExternalOutput" if name in debug else "Internal")

    scratch("xf", [D, S], F32)
    scratch("qf", [D, S], F32)
    scratch("kf", [D, S], F32)
    scratch("vf", [S, D], F32)
    scratch("x2", [D, S], F32)
    scratch("h2h", [D, S], BF16)
    scratch("h2l", [D, S], BF16)
    scratch("fsh", [DI, S], BF16)
    scratch("fsl", [DI, S], BF16)
    scratch("h1f", [D, S], F32)
    scratch("gtmp", [16, 128], F32)
    scratch("gwq", [D, DI], BF16)
    scratch("uwq", [D, DI], BF16)
    scratch("dwq", [DI, D], BF16)

    sbuf("identsb", [128, 128], F32)
    sbuf("onesf", [128, 128], F32)
    sbuf("modsb", [128, 96], F32)
    sbuf("vecs", [128, 6 * FC], F32)
    sbuf("sclT", [128, 2 * FCI + FC], F32)
    sbuf("gbrep", [128, D], F32)
    sbuf("onesr", [1, CH], F32)
    sbuf("consts", [128, 2], F32)

    for i in range(8):
        psum(f"ps{i}", [128, 512])

    p = P(nc)
    qguards = []
    if "s5" in stages or "s6" in stages:
        qb = {}
        for nm, shape, dt_ in (("q_f8", [128, 8 * 512], F8),
                               ("q_oblk", [128, 8 * 512], BF16)):
            g = nc.sbuf_tensor(nm, shape, dt_)
            qguards.append(g)
            qb[nm] = g.__enter__().ap()
        qltmp = {}
        for k in LADDER_TMPS:
            g = nc.sbuf_tensor("q_" + k, [128, 512], BF16)
            qguards.append(g)
            qltmp[k] = g.__enter__().ap()
        T["_q"] = QStream(p, T, qb["q_f8"], qb["q_oblk"], qltmp)
    if "s0" in stages:
        build_s0(p, T, sbuf_scoped)
        free_scope()
    if "s1" in stages:
        build_s1(p, T, sbuf_scoped)
        free_scope()
    if "s2" in stages:
        build_s2(p, T, sbuf_scoped)
        free_scope()
    if "s3" in stages:
        build_s3(p, T, sbuf_scoped)
        free_scope()
    if "s4" in stages:
        build_s4(p, T, sbuf_scoped)
        free_scope()
    if "_q" in T:
        T["_q"].flush()
        p.barrier(engines=("vector", "gpsimd", "tensor", "scalar", "sync"),
                  producers=["qdve", "d_qo"])
        while qguards:
            qguards.pop().__exit__(None, None, None)
    if "s5" in stages:
        build_s5(p, T, sbuf_scoped)
        free_scope()
    if "s6" in stages:
        build_s6(p, T, sbuf_scoped)
        free_scope()

    p.barrier(engines=("sync",))
    p.flush()
    return nc


def build_s0(p, T, sbuf):
    c16 = sbuf("c16", [16, 128], F32)
    siluc = sbuf("siluc", [128, 16], F32)
    wmt = sbuf("wmt", [128, 2 * D], F32)
    scrows = sbuf("scrows", [48, 3 * 128], F32)
    s0tmp = sbuf("s0tmp", [128, 32], F32)
    ident, modsb, vecs, sclT = T["identsb"], T["modsb"], T["vecs"], T["sclT"]

    p.dma("sync", ident[:], T["ident"][:], sem="d_const")
    p.op("vector", "memset", T["onesr"][:], 1.0)
    p.op("vector", "memset", T["onesf"][:], 1.0)
    p.op("vector", "memset", T["consts"][:, 0:1], EPS, inc=("dve", 1))
    p.dma("sync", c16[:], T["c"].rearrange("a (g f) -> (a g) f", f=128), sem="d_const")
    p.dma("sync", scrows[:, 0:128], T["gsc"].rearrange("a (g f) -> (a g) f", f=128), sem="d_const")
    p.dma("sync", scrows[:, 128:256], T["usc"].rearrange("a (g f) -> (a g) f", f=128), sem="d_const")
    p.dma("sync", scrows[0:16, 256:384], T["dsc"].rearrange("a (g f) -> (a g) f", f=128), sem="d_const")
    d_all = p.now("d_const")

    p.wait("scalar", "d_const", d_all)
    p.op("scalar", "activation", c16[:], c16[:], Act.Silu, inc=("act", 1))
    p.wait("tensor", "act")
    p.wait("tensor", "d_const", d_all)
    p.op("tensor", "transpose", T["ps0"][:, 0:16], c16[:], ident[0:16, 0:16],
         inc=("pe", 1))
    p.op("tensor", "transpose", T["ps1"][:, 0:48], scrows[:, 0:128],
         ident[0:48, 0:48], inc=("pe", 1))
    p.op("tensor", "transpose", T["ps2"][:, 0:48], scrows[:, 128:256],
         ident[0:48, 0:48], inc=("pe", 1))
    p.op("tensor", "transpose", T["ps3"][:, 0:16], scrows[0:16, 256:384],
         ident[0:16, 0:16], inc=("pe", 1))
    p.wait("scalar", "pe")
    p.op("scalar", "copy", siluc[:], T["ps0"][:, 0:16])
    p.op("scalar", "copy", sclT[:, 0:FCI], T["ps1"][:, 0:48])
    p.op("scalar", "copy", sclT[:, FCI:2 * FCI], T["ps2"][:, 0:48])
    p.op("scalar", "copy", sclT[:, 2 * FCI:2 * FCI + FC], T["ps3"][:, 0:16],
         inc=("act", 1))

    act_after_silu = p.now("act")
    evac_marks = {}
    for jc in range(96):
        par = jc % 2
        buf = wmt[:, par * D:(par + 1) * D]
        buf3 = f3(buf, 16, 128)
        src = fmaj(T["wmod"][:, jc * 128:(jc + 1) * 128])
        if jc >= 2:
            p.wait("sync", "act", evac_marks[jc - 2])
        p.dma("sync", buf3, src, sem=f"d_wm{par}")
        dmark = p.now(f"d_wm{par}")
        pst = T[f"ps{4 + par}"][:, 0:1]
        p.wait("tensor", f"d_wm{par}", dmark)
        if jc == 0:
            p.wait("tensor", "act", act_after_silu)
        if jc >= 2:
            p.wait("tensor", "act", evac_marks[jc - 2])
        for kc in range(16):
            p.op("tensor", "matmul", pst, buf[:, kc * 128:(kc + 1) * 128],
                 T["siluc"][:, kc:kc + 1], start=(kc == 0), stop=(kc == 15),
                 inc=(("pe", 1) if kc == 15 else ()))
        p.wait("scalar", "pe")
        evac_marks[jc] = p.op("scalar", "copy", modsb[:, jc:jc + 1], pst,
                              inc=("act", 1))

    sa, ca, ga = modsb[:, 0:16], modsb[:, 16:32], modsb[:, 32:48]
    sb_, cb, gb = modsb[:, 48:64], modsb[:, 64:80], modsb[:, 80:96]
    ca1, sav = vecs[:, 0:16], vecs[:, 16:32]
    cb1, sbv = vecs[:, 32:48], vecs[:, 48:64]
    gav, gbdsc = vecs[:, 64:80], vecs[:, 80:96]
    t0, t1 = s0tmp[:, 0:16], s0tmp[:, 16:32]
    p.wait("vector", "act")
    v = lambda m, *a, **k: p.op("vector", m, *a, **k)
    v("tensor_scalar", ca1, ca, 1.0, None, Alu.add)
    v("tensor_copy", sav, sa)
    v("tensor_scalar", cb1, cb, 1.0, None, Alu.add)
    v("tensor_copy", sbv, sb_)
    def vs(m, *a):
        p.op("vector", m, *a, inc=("dve", 1))
        p.wait("vector", "dve")
    for g_in, g_out, mul_dsc in ((ga, gav, False), (gb, gbdsc, True)):
        vs("scalar_tensor_tensor", t0, g_in, -1.0, g_in, Alu.mult, Alu.max)
        vs("tensor_single_scalar", t0, t0, 0.25, Alu.max)
        vs("tensor_scalar", t1, g_in, 0.0, 2.0, Alu.is_ge, Alu.mult)
        vs("tensor_mul", t1, t1, t0)
        vs("tensor_sub", g_out, t1, t0)
        if mul_dsc:
            vs("tensor_mul", g_out, g_out, sclT[:, 2 * FCI:2 * FCI + FC])
    grow = sbuf("s0_grow", [16, 128], F32)
    growr = sbuf("s0_growr", [1, 2 * D], F32)
    p.wait("tensor", "dve")
    p.op("tensor", "transpose", T["ps0"][0:16, 0:128], gbdsc, ident[:],
         inc=("pe", 1))
    p.wait("scalar", "pe")
    p.op("scalar", "copy", grow[:], T["ps0"][0:16, 0:128], inc=("act", 1))
    p.wait("gpsimd", "act")
    p.dma("gpsimd", T["gtmp"][:], grow[:], sem="d_gb")
    p.wait("gpsimd", "d_gb")
    p.dma("gpsimd", growr[0:1, 0:D],
          T["gtmp"].rearrange("g f -> (g f)").unsqueeze(0), sem="d_gb")
    p.wait("tensor", "d_gb")
    for jj in range(4):
        if jj > 0:
            p.wait("tensor", "act")
        p.op("tensor", "matmul", T["ps1"][:, 0:512], T["onesr"][0:1, 0:128],
             growr[0:1, jj * 512:(jj + 1) * 512], start=True, stop=True,
             inc=("pe", 1))
        p.wait("scalar", "pe")
        p.op("scalar", "copy", T["gbrep"][:, jj * 512:(jj + 1) * 512],
             T["ps1"][:, 0:512], inc=("act", 1))
    p.barrier()


def build_s1(p, T, sbuf):
    xin = sbuf("s1_xin", [128, 4 * D], F32)
    xfsb = sbuf("s1_xf", [128, FC * CH], F32)
    sq = sbuf("s1_sq", [128, 2 * CH], F32)
    mrep = sbuf("s1_mrep", [128, 2 * CH], F32)
    rrep = sbuf("s1_rrep", [128, 2 * CH], F32)
    vtmp = sbuf("s1_vtmp", [128, CH], F32)
    h1f = sbuf("s1_h1f", [128, CH], F32)
    h1st = sbuf("s1_h1st", [128, 2 * FC * CH], F32)
    ident, ones, vecs = T["identsb"], T["onesf"], T["vecs"]

    evac_done = {}
    stat2_done = {}
    apply_done = {}
    wr_marks = {}

    for ci in range(NCH):
        par = ci % 2
        xc = lambda g: xin[:, g * D:(g + 1) * D]
        if ci >= 1:
            p.wait("sync", "pe", stat2_done[ci * FC - 1])
        for g in range(4):
            p.dma("sync", xc(g), T["x"][(ci * 4 + g) * 128:(ci * 4 + g + 1) * 128, :])
        xload = p.now("d_sync")

        psS1 = T[f"ps{2 + 2 * par}"]
        psS2 = T[f"ps{3 + 2 * par}"]

        for j in range(FC):
            gi = ci * FC + j
            trb = T[f"ps{gi % 2}"]
            xft = xfsb[:, j * CH:(j + 1) * CH]
            sqt = sq[:, (j % 2) * CH:(j % 2 + 1) * CH]
            if j == 0:
                p.wait("tensor", "d_sync", xload)
            if gi - 2 in evac_done:
                p.wait("tensor", "act", evac_done[gi - 2])
            for g in range(4):
                p.op("tensor", "transpose", trb[:, g * 128:(g + 1) * 128],
                     xc(g)[:, j * 128:(j + 1) * 128], ident[:],
                     inc=(("pe", 1) if g == 3 else ()))
            tr_pe = p.now("pe")
            p.wait("scalar", "pe", tr_pe)
            if (ci - 1, j) in apply_done:
                p.wait("scalar", "dve", apply_done[(ci - 1, j)])
            if j == 0 and ci - 1 in wr_marks:
                p.wait("scalar", f"d_wr{(ci - 1) % 2}", wr_marks[ci - 1])
            if gi - 2 in stat2_done:
                p.wait("scalar", "pe", stat2_done[gi - 2])
            p.op("scalar", "copy", xft, trb[:, 0:CH])
            evac_done[gi] = p.op("scalar", "activation", sqt, xft, Act.Square,
                                 inc=("act", 1))
            p.wait("tensor", "act", evac_done[gi])
            if ci >= 2 and j == 0:
                p.wait("tensor", "dve", apply_done[(ci - 2, FC - 1)])
            p.op("tensor", "matmul", psS1[:], ones[:], xft,
                 start=(j == 0), stop=(j == FC - 1))
            stat2_done[gi] = p.op("tensor", "matmul", psS2[:], ones[:], sqt,
                                  start=(j == 0), stop=(j == FC - 1),
                                  inc=("pe", 1))

        mr = mrep[:, par * CH:(par + 1) * CH]
        rr = rrep[:, par * CH:(par + 1) * CH]
        p.wait("scalar", "pe", stat2_done[ci * FC + FC - 1])
        if ci >= 2:
            p.wait("scalar", "dve", apply_done[(ci - 2, FC - 1)])
        p.op("scalar", "activation", mr, psS1[:], Act.Copy, scale=1.0 / D,
             inc=("act", 1))
        mr_act = p.now("act")
        p.wait("vector", "act", mr_act)
        p.op("vector", "tensor_mul", vtmp[:], mr, mr)
        p.op("vector", "scalar_tensor_tensor", rr, psS2[:], 1.0 / D, vtmp[:],
             Alu.mult, Alu.subtract, inc=("dve", 1))
        p.wait("scalar", "dve")
        p.op("scalar", "activation", rr, rr, Act.Sqrt,
             bias=T["consts"][:, 0:1], inc=("act", 1))
        p.wait("vector", "act")
        p.op("vector", "reciprocal", rr, rr, inc=("dve", 1))

        for j in range(FC):
            xft = xfsb[:, j * CH:(j + 1) * CH]
            hst = h1st[:, (par * FC + j) * CH:(par * FC + j + 1) * CH]
            if j == 0 and ci - 2 in wr_marks:
                p.wait("vector", f"d_wr{ci % 2}", wr_marks[ci - 2])
            p.op("vector", "tensor_sub", h1f[:], xft, mr)
            p.op("vector", "tensor_mul", h1f[:], h1f[:], rr)
            p.op("vector", "tensor_scalar", hst, h1f[:],
                 vecs[:, 0 + j:1 + j], vecs[:, 16 + j:17 + j],
                 Alu.mult, Alu.add, inc=("dve", 1))
            apply_done[(ci, j)] = p.now("dve")

        p.wait("gpsimd", "act", evac_done[ci * FC + FC - 1])
        p.dma("gpsimd", fmaj(T["xf"][:, ci * CH:(ci + 1) * CH]),
              f3(xfsb[:], FC, CH), sem=f"d_wr{ci % 2}")
        p.wait("gpsimd", "dve", apply_done[(ci, FC - 1)])
        wr_marks[ci] = p.dma("gpsimd", fmaj(T["h1f"][:, ci * CH:(ci + 1) * CH]),
              f3(h1st[:, par * FC * CH:(par + 1) * FC * CH], FC, CH),
              sem=f"d_wr{ci % 2}")
        if "_q" in T:
            for _ in range(3):
                T["_q"].step()

    p.barrier()


def build_s2(p, T, sbuf):
    QW = 512
    wst = sbuf("s2_wst", [128, 2 * D], F32)
    wt = {w: sbuf(f"s2_w{w}", [128, FC * QW], F32) for w in ("q", "k", "v")}
    h1t = sbuf("s2_h1", [128, FC * CH], F32)
    stg = sbuf("s2_stg", [128, 4 * CH], F32)
    br = sbuf("biasrows", [1, 3 * 512], F32)
    onesr = T["onesr"]

    tr_evac = {}
    acc_marks = {}
    stg_dma = 0
    h1_use = 0
    wst_free = 0
    gbank = 0
    WD = {"q": T["wq"], "k": T["wk"], "v": T["wv"]}
    OUT = {"q": T["qf"], "k": T["kf"], "v": T["vf"]}

    for qi in range(4):
        if wst_free:
            p.wait("sync", "pe", wst_free)
        p.dma("sync", br[0:1, 0:512], T["bq"][0:1, qi * 512:(qi + 1) * 512],
              sem="d_br")
        p.dma("sync", br[0:1, 512:1024], T["bk"][0:1, qi * 512:(qi + 1) * 512],
              sem="d_br")
        p.dma("sync", br[0:1, 1024:1536], T["bv"][0:1, qi * 512:(qi + 1) * 512],
              sem="d_br")
        br_ld = p.now("d_br")
        for wi, w in enumerate(("q", "k", "v")):
            for hf in range(2):
                if wst_free:
                    p.wait("sync", "pe", wst_free)
                p.dma("sync", f3(wst[:], 2, D),
                      WD[w][qi * QW + hf * 256:qi * QW + (hf + 1) * 256, :].rearrange(
                          "(cc pp) k -> cc pp k", pp=128).transpose([1, 0, 2]))
                wld = p.now("d_sync")
                for c2 in range(2):
                    cc = hf * 2 + c2
                    for kc in range(FC):
                        gt = ((qi * 3 + wi) * 2 + hf) * 32 + c2 * FC + kc
                        trb = T[f"ps{6 + gt % 2}"]
                        if c2 == 0 and kc == 0:
                            p.wait("tensor", "d_sync", wld)
                        if gt - 2 in tr_evac:
                            p.wait("tensor", "act", tr_evac[gt - 2])
                        p.op("tensor", "transpose", trb[:, 0:128],
                             wst[:, c2 * D + kc * 128:c2 * D + (kc + 1) * 128],
                             T["identsb"][:], inc=("pe", 1))
                        tp = p.now("pe")
                        p.wait("scalar", "pe", tp)
                        tr_evac[gt] = p.op(
                            "scalar", "copy",
                            wt[w][:, kc * QW + cc * 128:kc * QW + (cc + 1) * 128],
                            trb[:, 0:128], inc=("act", 1))
                wst_free = p.now("pe")

        for tc in range(NCH):
            if h1_use:
                p.wait("sync", "pe", h1_use)
            p.dma("sync", f3(h1t[:], FC, CH), fmaj(T["h1f"][:, tc * CH:(tc + 1) * CH]))
            h1ld = p.now("d_sync")
            p.wait("tensor", "d_sync", h1ld)
            if tc == 0:
                p.wait("tensor", "act", tr_evac[(qi * 3 + 2) * 64 + 63])

            for w in ("q", "k", "v"):
                for cc in range(4):
                    bank = T[f"ps{gbank % 6}"]
                    gbank += 1
                    if (gbank - 1) % 6 in acc_marks and gbank > 6:
                        p.wait("tensor", "act", acc_marks[(gbank - 1) % 6])
                    if w in ("q", "k"):
                        boff = (0 if w == "q" else 512) + cc * 128
                        if tc == 0 and cc == 0:
                            p.wait("tensor", "d_br", br_ld)
                        p.op("tensor", "matmul", bank[:], br[0:1, boff:boff + 128],
                             onesr[0:1, 0:CH], start=True, stop=False)
                        for kc in range(FC):
                            p.op("tensor", "matmul", bank[:],
                                 wt[w][:, kc * QW + cc * 128:kc * QW + (cc + 1) * 128],
                                 h1t[:, kc * CH:(kc + 1) * CH], start=False,
                                 stop=(kc == FC - 1),
                                 inc=(("pe", 1) if kc == FC - 1 else ()))
                    else:
                        if tc == 0 and cc == 0:
                            p.wait("tensor", "d_br", br_ld)
                        p.op("tensor", "matmul", bank[:], onesr[0:1, 0:128],
                             br[0:1, 1024:1024 + QW], start=True, stop=False)
                        for kc in range(FC):
                            p.op("tensor", "matmul", bank[:],
                                 h1t[:, kc * CH + cc * 128:kc * CH + (cc + 1) * 128],
                                 wt[w][:, kc * QW:(kc + 1) * QW], start=False,
                                 stop=(kc == FC - 1),
                                 inc=(("pe", 1) if kc == FC - 1 else ()))
                    mme = p.now("pe")
                    p.wait("scalar", "pe", mme)
                    if stg_dma:
                        p.wait("scalar", "d_gpsimd", stg_dma)
                    acc_marks[(gbank - 1) % 6] = p.op(
                        "scalar", "copy", stg[:, cc * CH:(cc + 1) * CH], bank[:],
                        inc=("act", 1))
                p.wait("gpsimd", "act", acc_marks[(gbank - 1) % 6])
                if w in ("q", "k"):
                    dd = OUT[w][qi * QW:(qi + 1) * QW, tc * CH:(tc + 1) * CH]
                    stg_dma = p.dma("gpsimd", fmaj(dd), f3(stg[:], 4, CH))
                else:
                    dd = OUT[w][tc * CH:(tc + 1) * CH, qi * QW:(qi + 1) * QW]
                    stg_dma = p.dma(
                        "gpsimd",
                        dd.rearrange("(ts pp) cc -> ts pp cc", pp=128).transpose([1, 0, 2]),
                        f3(stg[:], 4, CH))
                if "_q" in T:
                    for _ in range(3):
                        T["_q"].step()
            h1_use = p.now("pe")

    p.barrier()


S3_HEADS = None
S3_WINS = None
S3_DUMP = False


def build_s3(p, T, sbuf):
    heads = list(S3_HEADS if S3_HEADS is not None else range(NH))
    wins = list(S3_WINS if S3_WINS is not None else range(NW))
    qb = sbuf("s3_q", [128, 2 * S], F32)
    kb = sbuf("s3_k", [128, 2 * S], F32)
    vb = sbuf("s3_v", [128, 2 * S], F32)
    xfb = sbuf("s3_xf", [128, 2 * S], F32)
    x2row = sbuf("s3_x2", [128, 2 * S], F32)
    caussb = sbuf("s3_caus", [128, 2 * GS], F32)
    ssb = sbuf("s3_s", [128, 2 * GS], F32)
    nrm = sbuf("s3_nrm", [128, 2 * GS], F32)
    red = sbuf("s3_red", [128, 8], F32)
    aT = sbuf("s3_aT", [128, 2 * GS], F32)
    vecs = T["vecs"]

    p.dma("sync", caussb[:], T["caus"][:], sem="d_h0")
    ld_marks = {}
    hpar = {}
    head_done = {}
    st_marks = {}
    sc_ev = {}
    tr_ev = {}
    av_ev = {}
    load_order = []

    def load_head(h):
        par = len(load_order) % 2
        hpar[h] = par
        load_order.append(h)
        sl = slice(par * S, (par + 1) * S)
        if len(load_order) > 2:
            pe_m, dve_m = head_done[load_order[-3]]
            p.wait("sync", "pe", pe_m)
            p.wait("sync", "dve", dve_m)
        hs = f"d_h{par}"
        p.dma("sync", qb[:, sl], T["qf"][h * 128:(h + 1) * 128, :], sem=hs)
        p.dma("sync", kb[:, sl], T["kf"][h * 128:(h + 1) * 128, :], sem=hs)
        p.dma("sync", f3(vb[:, sl], 32, 128),
              T["vf"][:, h * 128:(h + 1) * 128].rearrange(
                  "(ts pp) c -> ts pp c", pp=128).transpose([1, 0, 2]), sem=hs)
        p.dma("sync", xfb[:, sl], T["xf"][h * 128:(h + 1) * 128, :], sem=hs)
        ld_marks[h] = p.now(hs)

    for h0 in heads[:2]:
        load_head(h0)
    SC = float(1.0 / np.sqrt(128.0))

    for hi_, h in enumerate(heads):
        par = hpar[h]
        off = par * S
        shift = 128 if h >= 8 else 0
        p.wait("tensor", f"d_h{par}", ld_marks[h])
        gav = vecs[:, 64 + h:65 + h]

        for w in wins:
            b = w * GS + shift
            wrap = (b + 2 * GS > S) or (b + GS > S)
            kcol = [(b % S), ((b + 128) % S)]
            for qc in range(2):
                qo = (b + qc * 128) % S
                sb_i = (w * 2 + qc) % 4
                bank = T[f"ps{sb_i}"]
                if sb_i in sc_ev:
                    p.wait("tensor", "dve", sc_ev[sb_i])
                if not wrap:
                    p.op("tensor", "matmul", bank[:, 0:GS],
                         qb[:, off + qo:off + qo + 128],
                         kb[:, off + b:off + b + GS],
                         start=True, stop=True, inc=("pe", 1))
                else:
                    for kci, ko in enumerate(kcol):
                        p.op("tensor", "matmul",
                             bank[:, kci * 128:(kci + 1) * 128],
                             qb[:, off + qo:off + qo + 128],
                             kb[:, off + ko:off + ko + 128],
                             start=True, stop=True,
                             inc=(("pe", 1) if kci == 1 else ()))
                mm = p.now("pe")
                sst = ssb[:, qc * GS:(qc + 1) * GS]
                p.wait("vector", "pe", mm)
                p.op("vector", "scalar_tensor_tensor", sst, bank[:, 0:GS], SC,
                     caussb[:, qc * GS:(qc + 1) * GS], Alu.mult, Alu.add)
                nm = red[:, qc * 4:qc * 4 + 1]
                se = red[:, qc * 4 + 1:qc * 4 + 2]
                ri = red[:, qc * 4 + 2:qc * 4 + 3]
                p.op("vector", "reduce_max", nm, sst, AX, negate=True,
                     inc=("dve", 1))
                sc_ev[sb_i] = p.now("dve")
                p.wait("scalar", "dve")
                p.op("scalar", "activation", sst, sst, Act.Exp, bias=nm,
                     accum_out=se, inc=("act", 1))
                p.wait("vector", "act")
                p.op("vector", "reciprocal", ri, se, inc=("dve", 1))
                p.wait("vector", "dve")
                p.op("vector", "tensor_scalar", nrm[:, qc * GS:(qc + 1) * GS],
                     sst, ri, None, Alu.mult, inc=("dve", 1))
                nrm_dve = p.now("dve")
                for kci in range(2):
                    tb_i = 4 + kci
                    tbank = T[f"ps{tb_i}"]
                    if tb_i in tr_ev:
                        p.wait("tensor", "act", tr_ev[tb_i])
                    p.wait("tensor", "dve", nrm_dve)
                    p.op("tensor", "transpose", tbank[:, 0:128],
                         nrm[:, qc * GS + kci * 128:qc * GS + (kci + 1) * 128],
                         T["identsb"][:], inc=("pe", 1))
                    tp = p.now("pe")
                    p.wait("scalar", "pe", tp)
                    tr_ev[tb_i] = p.op(
                        "scalar", "copy",
                        aT[:, kci * GS + qc * 128:kci * GS + (qc + 1) * 128],
                        tbank[:, 0:128], inc=("act", 1))
            ab_i = 6 + w % 2
            abank = T[f"ps{ab_i}"]
            if ab_i in av_ev:
                p.wait("tensor", "dve", av_ev[ab_i])
            p.wait("tensor", "act", tr_ev[5])
            for kci, ko in enumerate(kcol):
                ts = ko // 128
                p.op("tensor", "matmul", abank[:, 0:GS],
                     vb[:, off + ts * 128:off + (ts + 1) * 128],
                     aT[:, kci * GS:(kci + 1) * GS],
                     start=(kci == 0), stop=(kci == 1),
                     inc=(("pe", 1) if kci == 1 else ()))
            avm = p.now("pe")
            p.wait("vector", "pe", avm)
            for piece in range(2):
                co = (b + piece * 128) % S
                p.op("vector", "scalar_tensor_tensor",
                     x2row[:, off + co:off + co + 128],
                     abank[:, piece * 128:(piece + 1) * 128], gav,
                     xfb[:, off + co:off + co + 128], Alu.mult, Alu.add,
                     inc=(("dve", 1) if piece == 1 else ()))
            av_ev[ab_i] = p.now("dve")
            if "_q" in T:
                T["_q"].step()

        p.wait("gpsimd", "dve", av_ev[6 + wins[-1] % 2])
        st_marks[h] = p.dma("gpsimd", T["x2"][h * 128:(h + 1) * 128, :],
                            x2row[:, off:off + S])
        head_done[h] = (p.now("pe"), p.now("dve"))
        if hi_ + 2 < len(heads):
            p.wait("vector", "d_gpsimd", st_marks[h])
            load_head(heads[hi_ + 2])

    p.barrier()


def build_s4(p, T, sbuf):
    x2sb = sbuf("s4_x2", [128, FC * CH], F32)
    sq = sbuf("s4_sq", [128, 2 * CH], F32)
    mrep = sbuf("s4_mrep", [128, 2 * CH], F32)
    rrep = sbuf("s4_rrep", [128, 2 * CH], F32)
    vtmp = sbuf("s4_vtmp", [128, CH], F32)
    h2f = sbuf("s4_h2f", [128, CH], F32)
    h2hi = sbuf("s4_h2hi", [128, 2 * FC * CH], BF16)
    h2lo = sbuf("s4_h2lo", [128, 2 * FC * CH], BF16)
    ones, vecs = T["onesf"], T["vecs"]

    stat_done = {}
    apply_done = {}
    wr_marks = {}
    sq_done = {}

    for ci in range(NCH):
        par = ci % 2
        if ci >= 1:
            p.wait("sync", "pe", stat_done[ci - 1])
            p.wait("sync", "dve", apply_done[(ci - 1, FC - 1)])
        p.dma("sync", f3(x2sb[:], FC, CH), fmaj(T["x2"][:, ci * CH:(ci + 1) * CH]))
        xload = p.now("d_sync")
        psS1 = T[f"ps{2 * par}"]
        psS2 = T[f"ps{2 * par + 1}"]
        for j in range(FC):
            gi = ci * FC + j
            xft = x2sb[:, j * CH:(j + 1) * CH]
            sqt = sq[:, (j % 2) * CH:(j % 2 + 1) * CH]
            if j == 0:
                p.wait("scalar", "d_sync", xload)
            if gi - 2 in sq_done:
                p.wait("scalar", "pe", sq_done[gi - 2])
            sqm = p.op("scalar", "activation", sqt, xft, Act.Square, inc=("act", 1))
            p.wait("tensor", "act", sqm)
            if ci >= 2 and j == 0:
                p.wait("tensor", "dve", apply_done[(ci - 2, FC - 1)])
            if j == 0:
                p.wait("tensor", "d_sync", xload)
            p.op("tensor", "matmul", psS1[:], ones[:], xft,
                 start=(j == 0), stop=(j == FC - 1))
            sq_done[gi] = p.op("tensor", "matmul", psS2[:], ones[:], sqt,
                               start=(j == 0), stop=(j == FC - 1), inc=("pe", 1))
        stat_done[ci] = sq_done[ci * FC + FC - 1]

        mr = mrep[:, par * CH:(par + 1) * CH]
        rr = rrep[:, par * CH:(par + 1) * CH]
        p.wait("scalar", "pe", stat_done[ci])
        if ci >= 2:
            p.wait("scalar", "dve", apply_done[(ci - 2, FC - 1)])
        p.op("scalar", "activation", mr, psS1[:], Act.Copy, scale=1.0 / D,
             inc=("act", 1))
        p.wait("vector", "act")
        p.op("vector", "tensor_mul", vtmp[:], mr, mr)
        p.op("vector", "scalar_tensor_tensor", rr, psS2[:], 1.0 / D, vtmp[:],
             Alu.mult, Alu.subtract, inc=("dve", 1))
        p.wait("scalar", "dve")
        p.op("scalar", "activation", rr, rr, Act.Sqrt, bias=T["consts"][:, 0:1],
             inc=("act", 1))
        p.wait("vector", "act")
        p.op("vector", "reciprocal", rr, rr, inc=("dve", 1))

        for j in range(FC):
            xft = x2sb[:, j * CH:(j + 1) * CH]
            hhi = h2hi[:, (par * FC + j) * CH:(par * FC + j + 1) * CH]
            hlo = h2lo[:, (par * FC + j) * CH:(par * FC + j + 1) * CH]
            if j == 0 and ci - 2 in wr_marks:
                p.wait("vector", f"d_w4{ci % 2}", wr_marks[ci - 2])
            p.op("vector", "tensor_sub", h2f[:], xft, mr)
            p.op("vector", "tensor_mul", h2f[:], h2f[:], rr)
            p.op("vector", "tensor_scalar", h2f[:], h2f[:],
                 vecs[:, 32 + j:33 + j], vecs[:, 48 + j:49 + j], Alu.mult, Alu.add)
            p.op("vector", "tensor_copy", hhi, h2f[:])
            p.op("vector", "tensor_tensor", hlo, h2f[:], hhi, Alu.subtract,
                 inc=("dve", 1))
            apply_done[(ci, j)] = p.now("dve")

        p.wait("gpsimd", "dve", apply_done[(ci, FC - 1)])
        p.dma("gpsimd", fmaj(T["h2h"][:, ci * CH:(ci + 1) * CH]),
              f3(h2hi[:, par * FC * CH:(par + 1) * FC * CH], FC, CH),
              sem=f"d_w4{ci % 2}")
        wr_marks[ci] = p.dma("gpsimd", fmaj(T["h2l"][:, ci * CH:(ci + 1) * CH]),
                             f3(h2lo[:, par * FC * CH:(par + 1) * FC * CH], FC, CH),
                             sem=f"d_w4{ci % 2}")
        if "_q" in T:
            for _ in range(2):
                T["_q"].step()

    p.barrier()


PW = 1536
NP = DI // PW
LCH = 768


def build_s5(p, T, sbuf):
    gq = sbuf("s5_gq", [128, FC * PW], BF16)
    uq = sbuf("s5_uq", [128, FC * PW], BF16)
    h2ht = sbuf("s5_h2h", [128, 2 * FC * CH], BF16)
    h2lt = sbuf("s5_h2l", [128, 2 * FC * CH], BF16)
    gf = sbuf("s5_gf", [128, CH], F32)
    ff = sbuf("s5_ff", [128, CH], F32)
    fsh = sbuf("s5_fsh", [128, 12 * CH], BF16)
    fsl = sbuf("s5_fsl", [128, 12 * CH], BF16)
    sclT = T["sclT"]

    acc_ev = {}
    st_marks = {}
    h2_use = {}
    ld_marks = {}
    gbank = 0

    def load_h2(tc):
        tpar = tc % 2
        if tc >= 2:
            p.wait("sync", "pe", h2_use[tc - 2])
        sl = slice(tpar * FC * CH, (tpar + 1) * FC * CH)
        hs = f"d_h2{tpar}"
        p.dma("sync", f3(h2ht[:, sl], FC, CH),
              fmaj(T["h2h"][:, tc * CH:(tc + 1) * CH]), sem=hs)
        ld_marks[tc] = p.dma("sync", f3(h2lt[:, sl], FC, CH),
                             fmaj(T["h2l"][:, tc * CH:(tc + 1) * CH]), sem=hs)

    for pi in range(NP):
        if pi > 0:
            p.wait("sync", "pe", h2_use[NCH - 1])
        for wn, dst in (("gwq", gq), ("uwq", uq)):
            p.dma("sync", f3(dst[:], FC, PW),
                  T[wn][:, pi * PW:(pi + 1) * PW].rearrange(
                      "(kc pp) c -> kc pp c", pp=128).transpose([1, 0, 2]),
                  sem="d_wq")
        wq_ld = p.now("d_wq")
        h2_use.clear()
        ld_marks.clear()
        load_h2(0)
        load_h2(1)

        for tc in range(NCH):
            tpar = tc % 2
            toff = tpar * FC * CH
            p.wait("tensor", f"d_h2{tpar}", ld_marks[tc])
            if tc == 0:
                p.wait("tensor", "d_wq", wq_ld)
            for cc in range(PW // 128):
                gcol = pi * (PW // 128) + cc
                bg = T[f"ps{2 * (gbank % 2)}"]
                bu = T[f"ps{2 * (gbank % 2) + 1}"]
                gbank += 1
                if 2 * ((gbank - 1) % 2) in acc_ev and gbank > 2:
                    p.wait("tensor", "dve", acc_ev[2 * ((gbank - 1) % 2)])
                for wq_t, bank in ((gq, bg), (uq, bu)):
                    for kc in range(FC):
                        wt_ = wq_t[:, kc * PW + cc * 128:kc * PW + (cc + 1) * 128]
                        p.op("tensor", "matmul", bank[:], wt_,
                             h2ht[:, toff + kc * CH:toff + (kc + 1) * CH],
                             start=(kc == 0), stop=False)
                        p.op("tensor", "matmul", bank[:], wt_,
                             h2lt[:, toff + kc * CH:toff + (kc + 1) * CH],
                             start=False, stop=(kc == FC - 1),
                             inc=(("pe", 1) if kc == FC - 1 else ()))
                mme = p.now("pe")
                p.wait("scalar", "pe", mme)
                p.op("scalar", "activation", gf[:], bg[:], Act.Silu,
                     scale=sclT[:, gcol:gcol + 1], inc=("act", 1))
                p.wait("vector", "act")
                if cc == 0 and st_marks:
                    p.wait("vector", "d_gpsimd", max(st_marks.values()))
                p.op("vector", "scalar_tensor_tensor", ff[:], bu[:],
                     sclT[:, FCI + gcol:FCI + gcol + 1], gf[:], Alu.mult, Alu.mult)
                p.op("vector", "tensor_copy", fsh[:, cc * CH:(cc + 1) * CH], ff[:])
                p.op("vector", "tensor_tensor", fsl[:, cc * CH:(cc + 1) * CH],
                     ff[:], fsh[:, cc * CH:(cc + 1) * CH], Alu.subtract,
                     inc=("dve", 1))
                acc_ev[2 * ((gbank - 1) % 2)] = p.now("dve")
            p.wait("gpsimd", "dve", acc_ev[2 * ((gbank - 1) % 2)])
            dh = T["fsh"][pi * PW:(pi + 1) * PW, tc * CH:(tc + 1) * CH]
            dl = T["fsl"][pi * PW:(pi + 1) * PW, tc * CH:(tc + 1) * CH]
            p.dma("gpsimd", fmaj(dh), f3(fsh[:], 12, CH))
            st_marks[(pi, tc)] = p.dma("gpsimd", fmaj(dl), f3(fsl[:], 12, CH))
            h2_use[tc] = p.now("pe")
            if tc + 2 < NCH:
                load_h2(tc + 2)

    p.barrier()


C6 = 256
N6 = S // C6


def build_s6(p, T, sbuf):
    HW_ = 1024
    C6 = 128
    N6 = S // C6
    dwq = sbuf("s6_dwq", [128, FCI * HW_], BF16)
    fsht = sbuf("s6_fsh", [128, 2 * FCI * C6], BF16)
    fslt = sbuf("s6_fsl", [128, 2 * FCI * C6], BF16)
    x2sb = sbuf("s6_x2", [128, FC * C6], F32)
    x2t = sbuf("s6_x2t", [128, D], F32)
    ost = sbuf("s6_ost", [128, HW_], F32)
    gbrep = T["gbrep"]
    ident = T["identsb"]

    fs_use = 0
    lad_done = 0
    acc_ev = {}
    tr_ev = {}
    st_marks = {}
    fld_marks = {}
    x2tr_done = {}

    for half in range(2):
        if half > 0:
            p.wait("sync", "pe", fs_use)
        lad_done = p.dma("sync", f3(dwq[:], FCI, HW_),
              T["dwq"][:, half * HW_:(half + 1) * HW_].rearrange(
                  "(kc pp) c -> kc pp c", pp=128).transpose([1, 0, 2]),
              sem="d_dwq")

        for tc in range(N6):
            tpar = tc % 2
            if tc >= 2 or half > 0:
                prev = fld_marks.get((half, tc - 2), fld_marks.get((half - 1, N6 - 2 + tc), 0))
                p.wait("sync", "pe", fs_use)
            p.dma("sync", f3(fsht[:, tpar * FCI * C6:(tpar + 1) * FCI * C6], FCI, C6),
                  fmaj(T["fsh"][:, tc * C6:(tc + 1) * C6]))
            fld_marks[(half, tc)] = p.dma(
                "sync", f3(fslt[:, tpar * FCI * C6:(tpar + 1) * FCI * C6], FCI, C6),
                fmaj(T["fsl"][:, tc * C6:(tc + 1) * C6]))
            fld = p.now("d_sync")
            if x2tr_done:
                p.wait("scalar", "pe", max(x2tr_done.values()))
            p.dma("scalar", f3(x2sb[:], FC, C6), fmaj(T["x2"][:, tc * C6:(tc + 1) * C6]))
            x2ld = p.now("d_scalar")
            p.wait("tensor", "d_scalar", x2ld)
            for jf in range(FC):
                tb = T[f"ps{4 + jf % 2}"]
                if (4 + jf % 2) in tr_ev:
                    p.wait("tensor", "act", tr_ev[4 + jf % 2])
                p.op("tensor", "transpose", tb[:, 0:128],
                     x2sb[:, jf * C6:(jf + 1) * C6], ident[:], inc=("pe", 1))
                tp = p.now("pe")
                p.wait("scalar", "pe", tp)
                if jf < 2 and acc_ev:
                    p.wait("scalar", "dve", max(acc_ev.values()))
                tr_ev[4 + jf % 2] = p.op("scalar", "copy",
                                         x2t[:, jf * 128:(jf + 1) * 128],
                                         tb[:, 0:128], inc=("act", 1))
            x2tr_done[tc % 2] = p.now("pe")
            x2t_act = p.now("act")

            p.wait("tensor", "d_sync", fld)
            if tc == 0:
                p.wait("tensor", "d_dwq", lad_done)
            for cg in range(2):
                bank = T[f"ps{cg}"]
                if cg in acc_ev and (tc > 0 or half > 0):
                    p.wait("tensor", "dve", acc_ev[cg])
                for kc in range(FCI):
                    fh = fsht[:, tpar * FCI * C6 + kc * C6:
                              tpar * FCI * C6 + (kc + 1) * C6]
                    fl = fslt[:, tpar * FCI * C6 + kc * C6:
                              tpar * FCI * C6 + (kc + 1) * C6]
                    wslice = dwq[:, kc * HW_ + cg * 512:kc * HW_ + (cg + 1) * 512]
                    p.op("tensor", "matmul", bank[:], fh, wslice,
                         start=(kc == 0), stop=False)
                    p.op("tensor", "matmul", bank[:], fl, wslice,
                         start=False, stop=(kc == FCI - 1),
                         inc=(("pe", 1) if kc == FCI - 1 else ()))
                mme = p.now("pe")
                p.wait("vector", "pe", mme)
                p.wait("vector", "act", x2t_act)
                if st_marks:
                    p.wait("vector", "d_gpsimd", max(st_marks.values()))
                fo = half * HW_ + cg * 512
                p.op("vector", "tensor_mul", ost[:, cg * 512:(cg + 1) * 512],
                     bank[:], gbrep[:, fo:fo + 512])
                p.op("vector", "tensor_tensor", ost[:, cg * 512:(cg + 1) * 512],
                     ost[:, cg * 512:(cg + 1) * 512], x2t[:, fo:fo + 512],
                     Alu.add, inc=("dve", 1))
                acc_ev[cg] = p.now("dve")
            p.wait("gpsimd", "dve", acc_ev[1])
            st_marks[(half, tc)] = p.dma(
                "gpsimd", T["out"][tc * C6:(tc + 1) * C6, half * HW_:(half + 1) * HW_],
                ost[:])
            fs_use = p.now("pe")

    p.barrier(engines=("sync", "gpsimd"))



_CAUS = np.concatenate([
    np.where(np.arange(2 * GS // 2)[None, :] > np.arange(128)[:, None], -1e9, 0.0),
    np.where(np.arange(2 * GS // 2)[None, :] > 128 + np.arange(128)[:, None], -1e9, 0.0),
], axis=1).astype(np.float32)
_IDENT = np.eye(128, dtype=np.float32)


def kernel(L0_attn_x, L0_attn_Wq, L0_attn_bq, L0_attn_Wk, L0_attn_bk,
           L0_attn_Wv, L0_attn_bv, L0_ffn_gate_w, L0_ffn_up_w, L0_ffn_down_w,
           L0_ffn_gate_scale, L0_ffn_up_scale, L0_ffn_down_scale, L0_c, L0_wmod):
    from concourse.bass_utils import run_bass_kernel_spmd
    nc = build_nc(("s0", "s1", "s2", "s3", "s4", "s5", "s6"))
    com = dict(
        wq=np.ascontiguousarray(L0_attn_Wq), wk=np.ascontiguousarray(L0_attn_Wk),
        wv=np.ascontiguousarray(L0_attn_Wv),
        bq=np.ascontiguousarray(L0_attn_bq)[None],
        bk=np.ascontiguousarray(L0_attn_bk)[None],
        bv=np.ascontiguousarray(L0_attn_bv)[None],
        gw=np.ascontiguousarray(L0_ffn_gate_w), uw=np.ascontiguousarray(L0_ffn_up_w),
        dw=np.ascontiguousarray(L0_ffn_down_w),
        gsc=np.ascontiguousarray(L0_ffn_gate_scale)[None],
        usc=np.ascontiguousarray(L0_ffn_up_scale)[None],
        dsc=np.ascontiguousarray(L0_ffn_down_scale)[None],
        wmod=np.ascontiguousarray(L0_wmod),
        ident=_IDENT, caus=_CAUS,
    )
    in_maps = [dict(com, x=np.ascontiguousarray(L0_attn_x[i]),
                    c=np.ascontiguousarray(L0_c[i:i + 1])) for i in range(8)]
    res = run_bass_kernel_spmd(nc, in_maps, core_ids=list(range(8)))
    return np.stack([r["out"] for r in res.results], axis=0)


# version 07 (final)
# speedup vs baseline: 1.1840x; 1.0715x over previous
import numpy as np
import concourse.bass as bass
import concourse.mybir as mybir

F32 = mybir.dt.float32
BF16 = mybir.dt.bfloat16
F8 = mybir.dt.float8e4
Alu = mybir.AluOpType
Act = mybir.ActivationFunctionType
AX = mybir.AxisListType.X

ENGINES = ("sync", "vector", "scalar", "tensor", "gpsimd")


def fmaj(ap2d, ncols=None):
    return ap2d.rearrange("(j pp) t -> j pp t", pp=128).transpose([1, 0, 2])


def f3(tile_ap, n, c):
    return tile_ap.rearrange("p (j t) -> p j t", t=c)

D = 2048
S = 4096
FC = D // 128
NH = 16
HD = 128
GS = 256
NW = S // GS
DI = 6144
FCI = DI // 128
CH = 512
NCH = S // CH
EPS = 1e-5


class P:

    def __init__(self, nc):
        self.nc = nc
        self.q = {e: [] for e in ENGINES}
        self.sems = {}
        self.cnt = {}

    def sem(self, name):
        if name not in self.sems:
            self.sems[name] = self.nc.alloc_semaphore(name)
            self.cnt[name] = 0
        return self.sems[name]

    def op(self, eng, meth, *args, inc=(), **kw):
        if inc and isinstance(inc[0], str):
            inc = (inc,)
        handles = []
        for s, n in inc:
            self.sem(s)
            self.cnt[s] += n
            handles.append((self.sems[s], n))

        def thunk(e, meth=meth, args=args, kw=kw, handles=handles):
            ins = getattr(e, meth)(*args, **kw)
            for h, n in handles:
                ins = ins.then_inc(h, n)

        self.q[eng].append(thunk)
        return self.cnt[inc[0][0]] if inc else None

    def dma(self, eng, out, in_, sem=None, **kw):
        semname = sem or ("d_" + eng)
        return self.op(eng, "dma_start", inc=((semname, 16),), out=out, in_=in_, **kw)

    def wait(self, eng, semname, val=None):
        self.sem(semname)
        v = self.cnt[semname] if val is None else val
        if v <= 0:
            return
        h = self.sems[semname]
        self.q[eng].append(lambda e, h=h, v=v: e.wait_ge(h, v))

    def now(self, semname):
        self.sem(semname)
        return self.cnt[semname]

    def barrier(self, engines=ENGINES, producers=None):
        prods = producers if producers is not None else list(self.cnt)
        vals = [(s, self.cnt[s]) for s in prods]
        for e in engines:
            for s, v in vals:
                if v > 0:
                    self.wait(e, s, v)

    def flush(self):
        with self.nc.Block() as block:
            for e in ENGINES:
                thunks = self.q[e]
                if not thunks:
                    continue
                getattr(block, e)(lambda eng, thunks=thunks: [t(eng) for t in thunks])
            self.q = {e: [] for e in ENGINES}


LADDER_TMPS = ("ls", "lt", "lz", "la1", "la2", "la3", "la4", "lp", "lr", "lb0")


def emit_e4m3(p, eng, out, x, tmp):
    s, t, z, a1, a2, a3, a4, pp, r, b0 = (tmp[k] for k in LADDER_TMPS)
    o = lambda m, *a: p.op(eng, m, *a)
    o("tensor_single_scalar", t, x, -512.0, Alu.mult)
    o("scalar_tensor_tensor", s, x, 512.0, t, Alu.mult, Alu.max)
    o("tensor_single_scalar", a1, s, 16.0, Alu.is_ge)
    o("tensor_single_scalar", a2, s, 32.0, Alu.is_ge)
    o("tensor_single_scalar", a3, s, 64.0, Alu.is_ge)
    o("tensor_single_scalar", a4, s, 128.0, Alu.is_ge)
    o("tensor_copy", z, s)
    for a in (a1, a2, a3, a4):
        o("tensor_scalar", pp, a, -0.5, 1.0, Alu.mult, Alu.add)
        o("tensor_mul", z, z, pp)
    o("tensor_copy", r, z)
    for thr, w in ((15.5, 16.0), (7.5, 8.0), (3.5, 4.0), (1.5, 2.0)):
        o("tensor_scalar", t, r, thr, w, Alu.is_ge, Alu.mult)
        o("tensor_sub", r, r, t)
    o("tensor_scalar", b0, r, 0.5, 1.0, Alu.is_ge, Alu.mult)
    o("tensor_sub", r, r, b0)
    o("tensor_sub", r, z, r)
    o("tensor_sub", t, r, z)
    o("tensor_single_scalar", t, t, 0.5, Alu.is_equal)
    o("tensor_mul", t, t, b0)
    o("tensor_sub", r, r, t)
    for a in (a1, a2, a3, a4):
        o("tensor_scalar", pp, a, 1.0, None, Alu.add)
        o("tensor_mul", r, r, pp)
    o("tensor_single_scalar", r, r, 2.0 ** -9, Alu.mult)
    o("tensor_scalar", t, x, 0.0, 2.0, Alu.is_ge, Alu.mult)
    o("tensor_mul", t, t, r)
    o("tensor_sub", out, t, r)



class QStream:

    def __init__(self, p, T, f8st, oblk, ltmp):
        self.p, self.T = p, T
        self.f8st, self.oblk, self.ltmp = f8st, oblk, ltmp
        self.blocks = []
        for name, R, C in (("gw", D, DI), ("uw", D, DI), ("dw", DI, D)):
            for rb in range(R // 1024):
                for cb in range(C // 512):
                    self.blocks.append((name, R, C, rb, cb))
        self.bi = 0
        self.kri = 0
        self.load_mark = 0
        self.prev_store = 0
        self.prev_ladder = 0

    def done(self):
        return self.bi >= len(self.blocks)

    def step(self):
        if self.done():
            return
        p, T = self.p, self.T
        name, R, C, rb, cb = self.blocks[self.bi]
        if self.kri == 0:
            if self.prev_ladder:
                p.wait("gpsimd", "qdve", self.prev_ladder)
            src = T[name][rb * 1024:(rb + 1) * 1024,
                          cb * 512:(cb + 1) * 512].rearrange(
                "(kr pp) c -> kr pp c", pp=128).transpose([1, 0, 2])
            self.load_mark = p.dma("gpsimd", f3(self.f8st[:], 8, 512), src,
                                   sem="d_qi")
            p.wait("vector", "d_qi", self.load_mark)
            if self.prev_store:
                p.wait("vector", "d_qo", self.prev_store)
        kr = self.kri
        sl = slice(kr * 512, (kr + 1) * 512)
        p.op("vector", "tensor_copy", self.oblk[:, sl], self.f8st[:, sl])
        emit_e4m3(p, "vector", self.oblk[:, sl], self.oblk[:, sl], self.ltmp)
        p.op("vector", "tensor_copy", self.ltmp["lt"][:, 0:1],
             self.oblk[:, kr * 512:kr * 512 + 1], inc=("qdve", 1))
        self.kri += 1
        if self.kri == 8:
            self.prev_ladder = p.now("qdve")
            p.wait("gpsimd", "qdve", self.prev_ladder)
            dst = T[name + "q"][rb * 1024:(rb + 1) * 1024,
                                cb * 512:(cb + 1) * 512].rearrange(
                "(kr pp) c -> kr pp c", pp=128).transpose([1, 0, 2])
            self.prev_store = p.dma("gpsimd", dst, f3(self.oblk[:], 8, 512),
                                    sem="d_qo")
            self.kri = 0
            self.bi += 1

    def flush(self):
        while not self.done():
            self.step()


def build_nc(stages=("s0", "s1"), debug=()):
    nc = bass.Bass("TRN2", target_bir_lowering=False,
                   detect_race_conditions=False)
    T = {}

    def dram(name, shape, dtype, kind="Internal"):
        T[name] = nc.dram_tensor(name, shape, dtype, kind=kind).ap()
        return T[name]

    scope_guards = []

    def sbuf(name, shape, dtype):
        T[name] = nc.alloc_sbuf_tensor(name, shape, dtype).ap()
        return T[name]

    def sbuf_scoped(name, shape, dtype):
        g = nc.sbuf_tensor(name, shape, dtype)
        h = g.__enter__()
        scope_guards.append(g)
        T[name] = h.ap()
        return T[name]

    def free_scope():
        while scope_guards:
            scope_guards.pop().__exit__(None, None, None)

    def psum(name, shape):
        T[name] = nc.alloc_psum_tensor(name, shape, F32).ap()
        return T[name]

    dram("x", [S, D], F32, "ExternalInput")
    dram("c", [1, D], F32, "ExternalInput")
    dram("wmod", [D, 6 * D], F32, "ExternalInput")
    for w in ("wq", "wk", "wv"):
        dram(w, [D, D], F32, "ExternalInput")
    for b in ("bq", "bk", "bv"):
        dram(b, [1, D], F32, "ExternalInput")
    dram("gw", [D, DI], F32, "ExternalInput")
    dram("uw", [D, DI], F32, "ExternalInput")
    dram("dw", [DI, D], F32, "ExternalInput")
    dram("gsc", [1, DI], F32, "ExternalInput")
    dram("usc", [1, DI], F32, "ExternalInput")
    dram("dsc", [1, D], F32, "ExternalInput")
    dram("ident", [128, 128], F32, "ExternalInput")
    dram("caus", [128, 2 * GS], F32, "ExternalInput")
    dram("out", [S, D], F32, "ExternalOutput")

    def scratch(name, shape, dtype):
        dram(name, shape, dtype, "ExternalOutput" if name in debug else "Internal")

    scratch("xf", [D, S], F32)
    scratch("qf", [D, S], F32)
    scratch("kf", [D, S], F32)
    scratch("vf", [S, D], F32)
    scratch("x2", [D, S], F32)
    scratch("h2h", [D, S], BF16)
    scratch("h2l", [D, S], BF16)
    scratch("fsh", [DI, S], BF16)
    scratch("fsl", [DI, S], BF16)
    scratch("h1f", [D, S], F32)
    scratch("gtmp", [16, 128], F32)
    scratch("gwq", [D, DI], BF16)
    scratch("uwq", [D, DI], BF16)
    scratch("dwq", [DI, D], BF16)

    sbuf("identsb", [128, 128], F32)
    sbuf("onesf", [128, 128], F32)
    sbuf("modsb", [128, 96], F32)
    sbuf("vecs", [128, 6 * FC], F32)
    sbuf("sclT", [128, 2 * FCI + FC], F32)
    sbuf("gbrep", [128, D], F32)
    sbuf("onesr", [1, CH], F32)
    sbuf("consts", [128, 2], F32)

    for i in range(8):
        psum(f"ps{i}", [128, 512])

    p = P(nc)
    qguards = []
    if "s5" in stages or "s6" in stages:
        qb = {}
        for nm, shape, dt_ in (("q_f8", [128, 8 * 512], F8),
                               ("q_oblk", [128, 8 * 512], BF16)):
            g = nc.sbuf_tensor(nm, shape, dt_)
            qguards.append(g)
            qb[nm] = g.__enter__().ap()
        qltmp = {}
        for k in LADDER_TMPS:
            g = nc.sbuf_tensor("q_" + k, [128, 512], BF16)
            qguards.append(g)
            qltmp[k] = g.__enter__().ap()
        T["_q"] = QStream(p, T, qb["q_f8"], qb["q_oblk"], qltmp)
    if "s0" in stages:
        build_s0(p, T, sbuf_scoped)
        free_scope()
    if "s1" in stages:
        build_s1(p, T, sbuf_scoped)
        free_scope()
    if "s2" in stages:
        build_s2(p, T, sbuf_scoped)
        free_scope()
    if "s3" in stages:
        build_s3(p, T, sbuf_scoped)
        free_scope()
    if "s4" in stages:
        build_s4(p, T, sbuf_scoped)
        free_scope()
    if "_q" in T:
        T["_q"].flush()
        p.barrier(engines=("vector", "gpsimd", "tensor", "scalar", "sync"),
                  producers=["qdve", "d_qo"])
        while qguards:
            qguards.pop().__exit__(None, None, None)
    if "s5" in stages:
        build_s5(p, T, sbuf_scoped)
        free_scope()
    if "s6" in stages:
        build_s6(p, T, sbuf_scoped)
        free_scope()

    p.barrier(engines=("sync",))
    p.flush()
    return nc


def build_s0(p, T, sbuf):
    c16 = sbuf("c16", [16, 128], F32)
    siluc = sbuf("siluc", [128, 16], F32)
    wmt = sbuf("wmt", [128, 2 * D], F32)
    scrows = sbuf("scrows", [48, 3 * 128], F32)
    s0tmp = sbuf("s0tmp", [128, 32], F32)
    ident, modsb, vecs, sclT = T["identsb"], T["modsb"], T["vecs"], T["sclT"]

    p.dma("sync", ident[:], T["ident"][:], sem="d_const")
    p.op("vector", "memset", T["onesr"][:], 1.0)
    p.op("vector", "memset", T["onesf"][:], 1.0)
    p.op("vector", "memset", T["consts"][:, 0:1], EPS, inc=("dve", 1))
    p.dma("sync", c16[:], T["c"].rearrange("a (g f) -> (a g) f", f=128), sem="d_const")
    p.dma("sync", scrows[:, 0:128], T["gsc"].rearrange("a (g f) -> (a g) f", f=128), sem="d_const")
    p.dma("sync", scrows[:, 128:256], T["usc"].rearrange("a (g f) -> (a g) f", f=128), sem="d_const")
    p.dma("sync", scrows[0:16, 256:384], T["dsc"].rearrange("a (g f) -> (a g) f", f=128), sem="d_const")
    d_all = p.now("d_const")

    p.wait("scalar", "d_const", d_all)
    p.op("scalar", "activation", c16[:], c16[:], Act.Silu, inc=("act", 1))
    p.wait("tensor", "act")
    p.wait("tensor", "d_const", d_all)
    p.op("tensor", "transpose", T["ps0"][:, 0:16], c16[:], ident[0:16, 0:16],
         inc=("pe", 1))
    p.op("tensor", "transpose", T["ps1"][:, 0:48], scrows[:, 0:128],
         ident[0:48, 0:48], inc=("pe", 1))
    p.op("tensor", "transpose", T["ps2"][:, 0:48], scrows[:, 128:256],
         ident[0:48, 0:48], inc=("pe", 1))
    p.op("tensor", "transpose", T["ps3"][:, 0:16], scrows[0:16, 256:384],
         ident[0:16, 0:16], inc=("pe", 1))
    p.wait("scalar", "pe")
    p.op("scalar", "copy", siluc[:], T["ps0"][:, 0:16])
    p.op("scalar", "copy", sclT[:, 0:FCI], T["ps1"][:, 0:48])
    p.op("scalar", "copy", sclT[:, FCI:2 * FCI], T["ps2"][:, 0:48])
    p.op("scalar", "copy", sclT[:, 2 * FCI:2 * FCI + FC], T["ps3"][:, 0:16],
         inc=("act", 1))

    act_after_silu = p.now("act")
    evac_marks = {}
    for jc in range(96):
        par = jc % 2
        buf = wmt[:, par * D:(par + 1) * D]
        buf3 = f3(buf, 16, 128)
        src = fmaj(T["wmod"][:, jc * 128:(jc + 1) * 128])
        if jc >= 2:
            p.wait("sync", "act", evac_marks[jc - 2])
        p.dma("sync", buf3, src, sem=f"d_wm{par}")
        dmark = p.now(f"d_wm{par}")
        pst = T[f"ps{4 + par}"][:, 0:1]
        p.wait("tensor", f"d_wm{par}", dmark)
        if jc == 0:
            p.wait("tensor", "act", act_after_silu)
        if jc >= 2:
            p.wait("tensor", "act", evac_marks[jc - 2])
        for kc in range(16):
            p.op("tensor", "matmul", pst, buf[:, kc * 128:(kc + 1) * 128],
                 T["siluc"][:, kc:kc + 1], start=(kc == 0), stop=(kc == 15),
                 inc=(("pe", 1) if kc == 15 else ()))
        p.wait("scalar", "pe")
        evac_marks[jc] = p.op("scalar", "copy", modsb[:, jc:jc + 1], pst,
                              inc=("act", 1))

    sa, ca, ga = modsb[:, 0:16], modsb[:, 16:32], modsb[:, 32:48]
    sb_, cb, gb = modsb[:, 48:64], modsb[:, 64:80], modsb[:, 80:96]
    ca1, sav = vecs[:, 0:16], vecs[:, 16:32]
    cb1, sbv = vecs[:, 32:48], vecs[:, 48:64]
    gav, gbdsc = vecs[:, 64:80], vecs[:, 80:96]
    t0, t1 = s0tmp[:, 0:16], s0tmp[:, 16:32]
    p.wait("vector", "act")
    v = lambda m, *a, **k: p.op("vector", m, *a, **k)
    v("tensor_scalar", ca1, ca, 1.0, None, Alu.add)
    v("tensor_copy", sav, sa)
    v("tensor_scalar", cb1, cb, 1.0, None, Alu.add)
    v("tensor_copy", sbv, sb_)
    def vs(m, *a):
        p.op("vector", m, *a, inc=("dve", 1))
        p.wait("vector", "dve")
    for g_in, g_out, mul_dsc in ((ga, gav, False), (gb, gbdsc, True)):
        vs("scalar_tensor_tensor", t0, g_in, -1.0, g_in, Alu.mult, Alu.max)
        vs("tensor_single_scalar", t0, t0, 0.25, Alu.max)
        vs("tensor_scalar", t1, g_in, 0.0, 2.0, Alu.is_ge, Alu.mult)
        vs("tensor_mul", t1, t1, t0)
        vs("tensor_sub", g_out, t1, t0)
        if mul_dsc:
            vs("tensor_mul", g_out, g_out, sclT[:, 2 * FCI:2 * FCI + FC])
    grow = sbuf("s0_grow", [16, 128], F32)
    growr = sbuf("s0_growr", [1, 2 * D], F32)
    p.wait("tensor", "dve")
    p.op("tensor", "transpose", T["ps0"][0:16, 0:128], gbdsc, ident[:],
         inc=("pe", 1))
    p.wait("scalar", "pe")
    p.op("scalar", "copy", grow[:], T["ps0"][0:16, 0:128], inc=("act", 1))
    p.wait("gpsimd", "act")
    p.dma("gpsimd", T["gtmp"][:], grow[:], sem="d_gb")
    p.wait("gpsimd", "d_gb")
    p.dma("gpsimd", growr[0:1, 0:D],
          T["gtmp"].rearrange("g f -> (g f)").unsqueeze(0), sem="d_gb")
    p.wait("tensor", "d_gb")
    for jj in range(4):
        if jj > 0:
            p.wait("tensor", "act")
        p.op("tensor", "matmul", T["ps1"][:, 0:512], T["onesr"][0:1, 0:128],
             growr[0:1, jj * 512:(jj + 1) * 512], start=True, stop=True,
             inc=("pe", 1))
        p.wait("scalar", "pe")
        p.op("scalar", "copy", T["gbrep"][:, jj * 512:(jj + 1) * 512],
             T["ps1"][:, 0:512], inc=("act", 1))
    p.barrier()


def build_s1(p, T, sbuf):
    xin = sbuf("s1_xin", [128, 4 * D], F32)
    xfsb = sbuf("s1_xf", [128, FC * CH], F32)
    sq = sbuf("s1_sq", [128, 2 * CH], F32)
    mrep = sbuf("s1_mrep", [128, 2 * CH], F32)
    rrep = sbuf("s1_rrep", [128, 2 * CH], F32)
    vtmp = sbuf("s1_vtmp", [128, CH], F32)
    h1f = sbuf("s1_h1f", [128, CH], F32)
    h1st = sbuf("s1_h1st", [128, 2 * FC * CH], F32)
    ident, ones, vecs = T["identsb"], T["onesf"], T["vecs"]

    evac_done = {}
    stat2_done = {}
    apply_done = {}
    wr_marks = {}

    for ci in range(NCH):
        par = ci % 2
        xc = lambda g: xin[:, g * D:(g + 1) * D]
        if ci >= 1:
            p.wait("sync", "pe", stat2_done[ci * FC - 1])
        for g in range(4):
            p.dma("sync", xc(g), T["x"][(ci * 4 + g) * 128:(ci * 4 + g + 1) * 128, :])
        xload = p.now("d_sync")

        psS1 = T[f"ps{2 + 2 * par}"]
        psS2 = T[f"ps{3 + 2 * par}"]

        for j in range(FC):
            gi = ci * FC + j
            trb = T[f"ps{gi % 2}"]
            xft = xfsb[:, j * CH:(j + 1) * CH]
            sqt = sq[:, (j % 2) * CH:(j % 2 + 1) * CH]
            if j == 0:
                p.wait("tensor", "d_sync", xload)
            if gi - 2 in evac_done:
                p.wait("tensor", "act", evac_done[gi - 2])
            for g in range(4):
                p.op("tensor", "transpose", trb[:, g * 128:(g + 1) * 128],
                     xc(g)[:, j * 128:(j + 1) * 128], ident[:],
                     inc=(("pe", 1) if g == 3 else ()))
            tr_pe = p.now("pe")
            p.wait("scalar", "pe", tr_pe)
            if (ci - 1, j) in apply_done:
                p.wait("scalar", "dve", apply_done[(ci - 1, j)])
            if j == 0 and ci - 1 in wr_marks:
                p.wait("scalar", f"d_wr{(ci - 1) % 2}", wr_marks[ci - 1])
            if gi - 2 in stat2_done:
                p.wait("scalar", "pe", stat2_done[gi - 2])
            p.op("scalar", "copy", xft, trb[:, 0:CH])
            evac_done[gi] = p.op("scalar", "activation", sqt, xft, Act.Square,
                                 inc=("act", 1))
            p.wait("tensor", "act", evac_done[gi])
            if ci >= 2 and j == 0:
                p.wait("tensor", "dve", apply_done[(ci - 2, FC - 1)])
            p.op("tensor", "matmul", psS1[:], ones[:], xft,
                 start=(j == 0), stop=(j == FC - 1))
            stat2_done[gi] = p.op("tensor", "matmul", psS2[:], ones[:], sqt,
                                  start=(j == 0), stop=(j == FC - 1),
                                  inc=("pe", 1))

        mr = mrep[:, par * CH:(par + 1) * CH]
        rr = rrep[:, par * CH:(par + 1) * CH]
        p.wait("scalar", "pe", stat2_done[ci * FC + FC - 1])
        if ci >= 2:
            p.wait("scalar", "dve", apply_done[(ci - 2, FC - 1)])
        p.op("scalar", "activation", mr, psS1[:], Act.Copy, scale=1.0 / D,
             inc=("act", 1))
        mr_act = p.now("act")
        p.wait("vector", "act", mr_act)
        p.op("vector", "tensor_mul", vtmp[:], mr, mr)
        p.op("vector", "scalar_tensor_tensor", rr, psS2[:], 1.0 / D, vtmp[:],
             Alu.mult, Alu.subtract, inc=("dve", 1))
        p.wait("scalar", "dve")
        p.op("scalar", "activation", rr, rr, Act.Sqrt,
             bias=T["consts"][:, 0:1], inc=("act", 1))
        p.wait("vector", "act")
        p.op("vector", "reciprocal", rr, rr, inc=("dve", 1))

        for j in range(FC):
            xft = xfsb[:, j * CH:(j + 1) * CH]
            hst = h1st[:, (par * FC + j) * CH:(par * FC + j + 1) * CH]
            if j == 0 and ci - 2 in wr_marks:
                p.wait("vector", f"d_wr{ci % 2}", wr_marks[ci - 2])
            p.op("vector", "tensor_sub", h1f[:], xft, mr)
            p.op("vector", "tensor_mul", h1f[:], h1f[:], rr)
            p.op("vector", "tensor_scalar", hst, h1f[:],
                 vecs[:, 0 + j:1 + j], vecs[:, 16 + j:17 + j],
                 Alu.mult, Alu.add, inc=("dve", 1))
            apply_done[(ci, j)] = p.now("dve")

        p.wait("gpsimd", "act", evac_done[ci * FC + FC - 1])
        p.dma("gpsimd", fmaj(T["xf"][:, ci * CH:(ci + 1) * CH]),
              f3(xfsb[:], FC, CH), sem=f"d_wr{ci % 2}")
        p.wait("gpsimd", "dve", apply_done[(ci, FC - 1)])
        wr_marks[ci] = p.dma("gpsimd", fmaj(T["h1f"][:, ci * CH:(ci + 1) * CH]),
              f3(h1st[:, par * FC * CH:(par + 1) * FC * CH], FC, CH),
              sem=f"d_wr{ci % 2}")
        if "_q" in T:
            for _ in range(3):
                T["_q"].step()

    p.barrier()


def build_s2(p, T, sbuf):
    QW = 512
    wst = sbuf("s2_wst", [128, 2 * D], F32)
    wt = {w: sbuf(f"s2_w{w}", [128, FC * QW], F32) for w in ("q", "k", "v")}
    h1t = sbuf("s2_h1", [128, FC * CH], F32)
    stg = sbuf("s2_stg", [128, 4 * CH], F32)
    br = sbuf("biasrows", [1, 3 * 512], F32)
    onesr = T["onesr"]

    tr_evac = {}
    acc_marks = {}
    stg_dma = 0
    h1_use = 0
    wst_free = 0
    gbank = 0
    WD = {"q": T["wq"], "k": T["wk"], "v": T["wv"]}
    OUT = {"q": T["qf"], "k": T["kf"], "v": T["vf"]}

    for qi in range(4):
        if wst_free:
            p.wait("sync", "pe", wst_free)
        p.dma("sync", br[0:1, 0:512], T["bq"][0:1, qi * 512:(qi + 1) * 512],
              sem="d_br")
        p.dma("sync", br[0:1, 512:1024], T["bk"][0:1, qi * 512:(qi + 1) * 512],
              sem="d_br")
        p.dma("sync", br[0:1, 1024:1536], T["bv"][0:1, qi * 512:(qi + 1) * 512],
              sem="d_br")
        br_ld = p.now("d_br")
        for wi, w in enumerate(("q", "k", "v")):
            for hf in range(2):
                if wst_free:
                    p.wait("sync", "pe", wst_free)
                p.dma("sync", f3(wst[:], 2, D),
                      WD[w][qi * QW + hf * 256:qi * QW + (hf + 1) * 256, :].rearrange(
                          "(cc pp) k -> cc pp k", pp=128).transpose([1, 0, 2]))
                wld = p.now("d_sync")
                for c2 in range(2):
                    cc = hf * 2 + c2
                    for kc in range(FC):
                        gt = ((qi * 3 + wi) * 2 + hf) * 32 + c2 * FC + kc
                        trb = T[f"ps{6 + gt % 2}"]
                        if c2 == 0 and kc == 0:
                            p.wait("tensor", "d_sync", wld)
                        if gt - 2 in tr_evac:
                            p.wait("tensor", "act", tr_evac[gt - 2])
                        p.op("tensor", "transpose", trb[:, 0:128],
                             wst[:, c2 * D + kc * 128:c2 * D + (kc + 1) * 128],
                             T["identsb"][:], inc=("pe", 1))
                        tp = p.now("pe")
                        p.wait("scalar", "pe", tp)
                        tr_evac[gt] = p.op(
                            "scalar", "copy",
                            wt[w][:, kc * QW + cc * 128:kc * QW + (cc + 1) * 128],
                            trb[:, 0:128], inc=("act", 1))
                wst_free = p.now("pe")

        for tc in range(NCH):
            if h1_use:
                p.wait("sync", "pe", h1_use)
            p.dma("sync", f3(h1t[:], FC, CH), fmaj(T["h1f"][:, tc * CH:(tc + 1) * CH]))
            h1ld = p.now("d_sync")
            p.wait("tensor", "d_sync", h1ld)
            if tc == 0:
                p.wait("tensor", "act", tr_evac[(qi * 3 + 2) * 64 + 63])

            for w in ("q", "k", "v"):
                for cc in range(4):
                    bank = T[f"ps{gbank % 6}"]
                    gbank += 1
                    if (gbank - 1) % 6 in acc_marks and gbank > 6:
                        p.wait("tensor", "act", acc_marks[(gbank - 1) % 6])
                    if w in ("q", "k"):
                        boff = (0 if w == "q" else 512) + cc * 128
                        if tc == 0 and cc == 0:
                            p.wait("tensor", "d_br", br_ld)
                        p.op("tensor", "matmul", bank[:], br[0:1, boff:boff + 128],
                             onesr[0:1, 0:CH], start=True, stop=False)
                        for kc in range(FC):
                            p.op("tensor", "matmul", bank[:],
                                 wt[w][:, kc * QW + cc * 128:kc * QW + (cc + 1) * 128],
                                 h1t[:, kc * CH:(kc + 1) * CH], start=False,
                                 stop=(kc == FC - 1),
                                 inc=(("pe", 1) if kc == FC - 1 else ()))
                    else:
                        if tc == 0 and cc == 0:
                            p.wait("tensor", "d_br", br_ld)
                        p.op("tensor", "matmul", bank[:], onesr[0:1, 0:128],
                             br[0:1, 1024:1024 + QW], start=True, stop=False)
                        for kc in range(FC):
                            p.op("tensor", "matmul", bank[:],
                                 h1t[:, kc * CH + cc * 128:kc * CH + (cc + 1) * 128],
                                 wt[w][:, kc * QW:(kc + 1) * QW], start=False,
                                 stop=(kc == FC - 1),
                                 inc=(("pe", 1) if kc == FC - 1 else ()))
                    mme = p.now("pe")
                    p.wait("scalar", "pe", mme)
                    if stg_dma:
                        p.wait("scalar", "d_gpsimd", stg_dma)
                    acc_marks[(gbank - 1) % 6] = p.op(
                        "scalar", "copy", stg[:, cc * CH:(cc + 1) * CH], bank[:],
                        inc=("act", 1))
                p.wait("gpsimd", "act", acc_marks[(gbank - 1) % 6])
                if w in ("q", "k"):
                    dd = OUT[w][qi * QW:(qi + 1) * QW, tc * CH:(tc + 1) * CH]
                    stg_dma = p.dma("gpsimd", fmaj(dd), f3(stg[:], 4, CH))
                else:
                    dd = OUT[w][tc * CH:(tc + 1) * CH, qi * QW:(qi + 1) * QW]
                    stg_dma = p.dma(
                        "gpsimd",
                        dd.rearrange("(ts pp) cc -> ts pp cc", pp=128).transpose([1, 0, 2]),
                        f3(stg[:], 4, CH))
                if "_q" in T:
                    for _ in range(3):
                        T["_q"].step()
            h1_use = p.now("pe")

    p.barrier()


S3_HEADS = None
S3_WINS = None
S3_DUMP = False


def build_s3(p, T, sbuf):
    heads = list(S3_HEADS if S3_HEADS is not None else range(NH))
    wins = list(S3_WINS if S3_WINS is not None else range(NW))
    qb = sbuf("s3_q", [128, 2 * S], F32)
    kb = sbuf("s3_k", [128, 2 * S], F32)
    vb = sbuf("s3_v", [128, 2 * S], F32)
    xfb = sbuf("s3_xf", [128, 2 * S], F32)
    x2row = sbuf("s3_x2", [128, 2 * S], F32)
    caussb = sbuf("s3_caus", [128, 2 * GS], F32)
    ssb = sbuf("s3_s", [128, 2 * 2 * GS], F32)
    nrm = sbuf("s3_nrm", [128, 2 * 2 * GS], F32)
    red = sbuf("s3_red", [128, 16], F32)
    aT = sbuf("s3_aT", [128, 2 * GS], F32)
    vecs = T["vecs"]

    p.dma("sync", caussb[:], T["caus"][:], sem="d_h0")
    ld_marks = {}
    hpar = {}
    head_done = {}
    st_marks = {}
    sc_ev = {}
    tr_ev = {}
    av_ev = {}
    load_order = []

    def load_head(h):
        par = len(load_order) % 2
        hpar[h] = par
        load_order.append(h)
        sl = slice(par * S, (par + 1) * S)
        if len(load_order) > 2:
            pe_m, dve_m = head_done[load_order[-3]]
            p.wait("sync", "pe", pe_m)
            p.wait("sync", "dve", dve_m)
        hs = f"d_h{par}"
        p.dma("sync", qb[:, sl], T["qf"][h * 128:(h + 1) * 128, :], sem=hs)
        p.dma("sync", kb[:, sl], T["kf"][h * 128:(h + 1) * 128, :], sem=hs)
        p.dma("sync", f3(vb[:, sl], 32, 128),
              T["vf"][:, h * 128:(h + 1) * 128].rearrange(
                  "(ts pp) c -> ts pp c", pp=128).transpose([1, 0, 2]), sem=hs)
        p.dma("sync", xfb[:, sl], T["xf"][h * 128:(h + 1) * 128, :], sem=hs)
        ld_marks[h] = p.now(hs)

    for h0 in heads[:2]:
        load_head(h0)
    SC = float(1.0 / np.sqrt(128.0))

    for hi_, h in enumerate(heads):
        par = hpar[h]
        off = par * S
        shift = 128 if h >= 8 else 0
        p.wait("tensor", f"d_h{par}", ld_marks[h])
        gav = vecs[:, 64 + h:65 + h]
        score_mark = {}
        nrm_mark = {}
        tr_mark = {}
        avm = {}

        def kcols(w):
            b = w * GS + shift
            return b, [(b % S), ((b + 128) % S)], (b + GS > S)

        def emit_scores(w):
            b, kcol, wrap = kcols(w)
            for qc in range(2):
                qo = (b + qc * 128) % S
                bank = T[f"ps{(w % 2) * 2 + qc}"]
                if ((w % 2) * 2 + qc) in sc_ev:
                    p.wait("tensor", "dve", sc_ev[(w % 2) * 2 + qc])
                if not wrap:
                    p.op("tensor", "matmul", bank[:, 0:GS],
                         qb[:, off + qo:off + qo + 128],
                         kb[:, off + b:off + b + GS],
                         start=True, stop=True,
                         inc=(("pe", 1) if qc == 1 else ()))
                else:
                    for kci, ko in enumerate(kcol):
                        p.op("tensor", "matmul",
                             bank[:, kci * 128:(kci + 1) * 128],
                             qb[:, off + qo:off + qo + 128],
                             kb[:, off + ko:off + ko + 128],
                             start=True, stop=True,
                             inc=(("pe", 1) if (qc == 1 and kci == 1) else ()))
            score_mark[w] = p.now("pe")

        def emit_softmax(w):
            wo = (w % 2) * 2 * GS
            p.wait("vector", "pe", score_mark[w])
            if w - 2 in tr_mark:
                p.wait("vector", "pe", tr_mark[w - 2])
            for qc in range(2):
                bank = T[f"ps{(w % 2) * 2 + qc}"]
                sst = ssb[:, wo + qc * GS:wo + (qc + 1) * GS]
                p.op("vector", "scalar_tensor_tensor", sst, bank[:, 0:GS], SC,
                     caussb[:, qc * GS:(qc + 1) * GS], Alu.mult, Alu.add)
                nm = red[:, (w % 2) * 8 + qc * 4:(w % 2) * 8 + qc * 4 + 1]
                se = red[:, (w % 2) * 8 + qc * 4 + 1:(w % 2) * 8 + qc * 4 + 2]
                ri = red[:, (w % 2) * 8 + qc * 4 + 2:(w % 2) * 8 + qc * 4 + 3]
                p.op("vector", "reduce_max", nm, sst, AX, negate=True,
                     inc=("dve", 1))
                if qc == 1:
                    sc_ev[(w % 2) * 2] = p.now("dve")
                    sc_ev[(w % 2) * 2 + 1] = p.now("dve")
                p.wait("scalar", "dve")
                p.op("scalar", "activation", sst, sst, Act.Exp, bias=nm,
                     accum_out=se, inc=("act", 1))
                p.wait("vector", "act")
                p.op("vector", "reciprocal", ri, se, inc=("dve", 1))
                p.wait("vector", "dve")
                p.op("vector", "tensor_scalar",
                     nrm[:, wo + qc * GS:wo + (qc + 1) * GS], sst, ri, None,
                     Alu.mult, inc=("dve", 1))
            nrm_mark[w] = p.now("dve")

        def emit_trav(w):
            b, kcol, wrap = kcols(w)
            wo = (w % 2) * 2 * GS
            for qc in range(2):
                for kci in range(2):
                    tb_i = 4 + kci
                    tbank = T[f"ps{tb_i}"]
                    if tb_i in tr_ev:
                        p.wait("tensor", "act", tr_ev[tb_i])
                    p.wait("tensor", "dve", nrm_mark[w])
                    p.op("tensor", "transpose", tbank[:, 0:128],
                         nrm[:, wo + qc * GS + kci * 128:wo + qc * GS + (kci + 1) * 128],
                         T["identsb"][:], inc=("pe", 1))
                    tp = p.now("pe")
                    p.wait("scalar", "pe", tp)
                    tr_ev[tb_i] = p.op(
                        "scalar", "copy",
                        aT[:, kci * GS + qc * 128:kci * GS + (qc + 1) * 128],
                        tbank[:, 0:128], inc=("act", 1))
            tr_mark[w] = p.now("pe")
            ab_i = 6 + w % 2
            abank = T[f"ps{ab_i}"]
            if ab_i in av_ev:
                p.wait("tensor", "dve", av_ev[ab_i])
            p.wait("tensor", "act", tr_ev[5])
            for kci, ko in enumerate(kcol):
                ts = ko // 128
                p.op("tensor", "matmul", abank[:, 0:GS],
                     vb[:, off + ts * 128:off + (ts + 1) * 128],
                     aT[:, kci * GS:(kci + 1) * GS],
                     start=(kci == 0), stop=(kci == 1),
                     inc=(("pe", 1) if kci == 1 else ()))
            avm[w] = p.now("pe")
            p.wait("vector", "pe", avm[w])
            for piece in range(2):
                co = (b + piece * 128) % S
                p.op("vector", "scalar_tensor_tensor",
                     x2row[:, off + co:off + co + 128],
                     abank[:, piece * 128:(piece + 1) * 128], gav,
                     xfb[:, off + co:off + co + 128], Alu.mult, Alu.add,
                     inc=(("dve", 1) if piece == 1 else ()))
            av_ev[ab_i] = p.now("dve")

        seq = wins + [None, None]
        for i2, w in enumerate(seq):
            if w is not None:
                emit_scores(w)
            if i2 >= 1 and seq[i2 - 1] is not None:
                emit_softmax(seq[i2 - 1])
            if i2 >= 2 and seq[i2 - 2] is not None:
                emit_trav(seq[i2 - 2])
                if "_q" in T:
                    T["_q"].step()

        p.wait("gpsimd", "dve", av_ev[6 + wins[-1] % 2])
        st_marks[h] = p.dma("gpsimd", T["x2"][h * 128:(h + 1) * 128, :],
                            x2row[:, off:off + S])
        head_done[h] = (p.now("pe"), p.now("dve"))
        if hi_ + 2 < len(heads):
            p.wait("vector", "d_gpsimd", st_marks[h])
            load_head(heads[hi_ + 2])

    p.barrier()


def build_s4(p, T, sbuf):
    x2sb = sbuf("s4_x2", [128, FC * CH], F32)
    sq = sbuf("s4_sq", [128, 2 * CH], F32)
    mrep = sbuf("s4_mrep", [128, 2 * CH], F32)
    rrep = sbuf("s4_rrep", [128, 2 * CH], F32)
    vtmp = sbuf("s4_vtmp", [128, CH], F32)
    h2f = sbuf("s4_h2f", [128, CH], F32)
    h2hi = sbuf("s4_h2hi", [128, 2 * FC * CH], BF16)
    h2lo = sbuf("s4_h2lo", [128, 2 * FC * CH], BF16)
    ones, vecs = T["onesf"], T["vecs"]

    stat_done = {}
    apply_done = {}
    wr_marks = {}
    sq_done = {}

    for ci in range(NCH):
        par = ci % 2
        if ci >= 1:
            p.wait("sync", "pe", stat_done[ci - 1])
            p.wait("sync", "dve", apply_done[(ci - 1, FC - 1)])
        p.dma("sync", f3(x2sb[:], FC, CH), fmaj(T["x2"][:, ci * CH:(ci + 1) * CH]))
        xload = p.now("d_sync")
        psS1 = T[f"ps{2 * par}"]
        psS2 = T[f"ps{2 * par + 1}"]
        for j in range(FC):
            gi = ci * FC + j
            xft = x2sb[:, j * CH:(j + 1) * CH]
            sqt = sq[:, (j % 2) * CH:(j % 2 + 1) * CH]
            if j == 0:
                p.wait("scalar", "d_sync", xload)
            if gi - 2 in sq_done:
                p.wait("scalar", "pe", sq_done[gi - 2])
            sqm = p.op("scalar", "activation", sqt, xft, Act.Square, inc=("act", 1))
            p.wait("tensor", "act", sqm)
            if ci >= 2 and j == 0:
                p.wait("tensor", "dve", apply_done[(ci - 2, FC - 1)])
            if j == 0:
                p.wait("tensor", "d_sync", xload)
            p.op("tensor", "matmul", psS1[:], ones[:], xft,
                 start=(j == 0), stop=(j == FC - 1))
            sq_done[gi] = p.op("tensor", "matmul", psS2[:], ones[:], sqt,
                               start=(j == 0), stop=(j == FC - 1), inc=("pe", 1))
        stat_done[ci] = sq_done[ci * FC + FC - 1]

        mr = mrep[:, par * CH:(par + 1) * CH]
        rr = rrep[:, par * CH:(par + 1) * CH]
        p.wait("scalar", "pe", stat_done[ci])
        if ci >= 2:
            p.wait("scalar", "dve", apply_done[(ci - 2, FC - 1)])
        p.op("scalar", "activation", mr, psS1[:], Act.Copy, scale=1.0 / D,
             inc=("act", 1))
        p.wait("vector", "act")
        p.op("vector", "tensor_mul", vtmp[:], mr, mr)
        p.op("vector", "scalar_tensor_tensor", rr, psS2[:], 1.0 / D, vtmp[:],
             Alu.mult, Alu.subtract, inc=("dve", 1))
        p.wait("scalar", "dve")
        p.op("scalar", "activation", rr, rr, Act.Sqrt, bias=T["consts"][:, 0:1],
             inc=("act", 1))
        p.wait("vector", "act")
        p.op("vector", "reciprocal", rr, rr, inc=("dve", 1))

        for j in range(FC):
            xft = x2sb[:, j * CH:(j + 1) * CH]
            hhi = h2hi[:, (par * FC + j) * CH:(par * FC + j + 1) * CH]
            hlo = h2lo[:, (par * FC + j) * CH:(par * FC + j + 1) * CH]
            if j == 0 and ci - 2 in wr_marks:
                p.wait("vector", f"d_w4{ci % 2}", wr_marks[ci - 2])
            p.op("vector", "tensor_sub", h2f[:], xft, mr)
            p.op("vector", "tensor_mul", h2f[:], h2f[:], rr)
            p.op("vector", "tensor_scalar", h2f[:], h2f[:],
                 vecs[:, 32 + j:33 + j], vecs[:, 48 + j:49 + j], Alu.mult, Alu.add)
            p.op("vector", "tensor_copy", hhi, h2f[:])
            p.op("vector", "tensor_tensor", hlo, h2f[:], hhi, Alu.subtract,
                 inc=("dve", 1))
            apply_done[(ci, j)] = p.now("dve")

        p.wait("gpsimd", "dve", apply_done[(ci, FC - 1)])
        p.dma("gpsimd", fmaj(T["h2h"][:, ci * CH:(ci + 1) * CH]),
              f3(h2hi[:, par * FC * CH:(par + 1) * FC * CH], FC, CH),
              sem=f"d_w4{ci % 2}")
        wr_marks[ci] = p.dma("gpsimd", fmaj(T["h2l"][:, ci * CH:(ci + 1) * CH]),
                             f3(h2lo[:, par * FC * CH:(par + 1) * FC * CH], FC, CH),
                             sem=f"d_w4{ci % 2}")
        if "_q" in T:
            for _ in range(2):
                T["_q"].step()

    p.barrier()


PW = 1536
NP = DI // PW
LCH = 768


def build_s5(p, T, sbuf):
    gq = sbuf("s5_gq", [128, FC * PW], BF16)
    uq = sbuf("s5_uq", [128, FC * PW], BF16)
    h2ht = sbuf("s5_h2h", [128, 2 * FC * CH], BF16)
    h2lt = sbuf("s5_h2l", [128, 2 * FC * CH], BF16)
    gf = sbuf("s5_gf", [128, CH], F32)
    ff = sbuf("s5_ff", [128, CH], F32)
    fsh = sbuf("s5_fsh", [128, 12 * CH], BF16)
    fsl = sbuf("s5_fsl", [128, 12 * CH], BF16)
    sclT = T["sclT"]

    acc_ev = {}
    st_marks = {}
    h2_use = {}
    ld_marks = {}
    gbank = 0

    def load_h2(tc):
        tpar = tc % 2
        if tc >= 2:
            p.wait("sync", "pe", h2_use[tc - 2])
        sl = slice(tpar * FC * CH, (tpar + 1) * FC * CH)
        hs = f"d_h2{tpar}"
        p.dma("sync", f3(h2ht[:, sl], FC, CH),
              fmaj(T["h2h"][:, tc * CH:(tc + 1) * CH]), sem=hs)
        ld_marks[tc] = p.dma("sync", f3(h2lt[:, sl], FC, CH),
                             fmaj(T["h2l"][:, tc * CH:(tc + 1) * CH]), sem=hs)

    for pi in range(NP):
        if pi > 0:
            p.wait("sync", "pe", h2_use[NCH - 1])
        for wn, dst in (("gwq", gq), ("uwq", uq)):
            p.dma("sync", f3(dst[:], FC, PW),
                  T[wn][:, pi * PW:(pi + 1) * PW].rearrange(
                      "(kc pp) c -> kc pp c", pp=128).transpose([1, 0, 2]),
                  sem="d_wq")
        wq_ld = p.now("d_wq")
        h2_use.clear()
        ld_marks.clear()
        load_h2(0)
        load_h2(1)

        for tc in range(NCH):
            tpar = tc % 2
            toff = tpar * FC * CH
            p.wait("tensor", f"d_h2{tpar}", ld_marks[tc])
            if tc == 0:
                p.wait("tensor", "d_wq", wq_ld)
            for cc in range(PW // 128):
                gcol = pi * (PW // 128) + cc
                bg = T[f"ps{2 * (gbank % 2)}"]
                bu = T[f"ps{2 * (gbank % 2) + 1}"]
                gbank += 1
                if 2 * ((gbank - 1) % 2) in acc_ev and gbank > 2:
                    p.wait("tensor", "dve", acc_ev[2 * ((gbank - 1) % 2)])
                for wq_t, bank in ((gq, bg), (uq, bu)):
                    for kc in range(FC):
                        wt_ = wq_t[:, kc * PW + cc * 128:kc * PW + (cc + 1) * 128]
                        p.op("tensor", "matmul", bank[:], wt_,
                             h2ht[:, toff + kc * CH:toff + (kc + 1) * CH],
                             start=(kc == 0), stop=False)
                        p.op("tensor", "matmul", bank[:], wt_,
                             h2lt[:, toff + kc * CH:toff + (kc + 1) * CH],
                             start=False, stop=(kc == FC - 1),
                             inc=(("pe", 1) if kc == FC - 1 else ()))
                mme = p.now("pe")
                p.wait("scalar", "pe", mme)
                p.op("scalar", "activation", gf[:], bg[:], Act.Silu,
                     scale=sclT[:, gcol:gcol + 1], inc=("act", 1))
                p.wait("vector", "act")
                if cc == 0 and st_marks:
                    p.wait("vector", "d_gpsimd", max(st_marks.values()))
                p.op("vector", "scalar_tensor_tensor", ff[:], bu[:],
                     sclT[:, FCI + gcol:FCI + gcol + 1], gf[:], Alu.mult, Alu.mult)
                p.op("vector", "tensor_copy", fsh[:, cc * CH:(cc + 1) * CH], ff[:])
                p.op("vector", "tensor_tensor", fsl[:, cc * CH:(cc + 1) * CH],
                     ff[:], fsh[:, cc * CH:(cc + 1) * CH], Alu.subtract,
                     inc=("dve", 1))
                acc_ev[2 * ((gbank - 1) % 2)] = p.now("dve")
            p.wait("gpsimd", "dve", acc_ev[2 * ((gbank - 1) % 2)])
            dh = T["fsh"][pi * PW:(pi + 1) * PW, tc * CH:(tc + 1) * CH]
            dl = T["fsl"][pi * PW:(pi + 1) * PW, tc * CH:(tc + 1) * CH]
            p.dma("gpsimd", fmaj(dh), f3(fsh[:], 12, CH))
            st_marks[(pi, tc)] = p.dma("gpsimd", fmaj(dl), f3(fsl[:], 12, CH))
            h2_use[tc] = p.now("pe")
            if tc + 2 < NCH:
                load_h2(tc + 2)

    p.barrier()


C6 = 256
N6 = S // C6


def build_s6(p, T, sbuf):
    HW_ = 1024
    C6 = 128
    N6 = S // C6
    dwq = sbuf("s6_dwq", [128, FCI * HW_], BF16)
    fsht = sbuf("s6_fsh", [128, 2 * FCI * C6], BF16)
    fslt = sbuf("s6_fsl", [128, 2 * FCI * C6], BF16)
    x2sb = sbuf("s6_x2", [128, FC * C6], F32)
    x2t = sbuf("s6_x2t", [128, D], F32)
    ost = sbuf("s6_ost", [128, HW_], F32)
    gbrep = T["gbrep"]
    ident = T["identsb"]

    fs_use = 0
    lad_done = 0
    acc_ev = {}
    tr_ev = {}
    st_marks = {}
    fld_marks = {}
    x2tr_done = {}

    for half in range(2):
        if half > 0:
            p.wait("sync", "pe", fs_use)
        lad_done = p.dma("sync", f3(dwq[:], FCI, HW_),
              T["dwq"][:, half * HW_:(half + 1) * HW_].rearrange(
                  "(kc pp) c -> kc pp c", pp=128).transpose([1, 0, 2]),
              sem="d_dwq")

        for tc in range(N6):
            tpar = tc % 2
            if tc >= 2 or half > 0:
                prev = fld_marks.get((half, tc - 2), fld_marks.get((half - 1, N6 - 2 + tc), 0))
                p.wait("sync", "pe", fs_use)
            p.dma("sync", f3(fsht[:, tpar * FCI * C6:(tpar + 1) * FCI * C6], FCI, C6),
                  fmaj(T["fsh"][:, tc * C6:(tc + 1) * C6]))
            fld_marks[(half, tc)] = p.dma(
                "sync", f3(fslt[:, tpar * FCI * C6:(tpar + 1) * FCI * C6], FCI, C6),
                fmaj(T["fsl"][:, tc * C6:(tc + 1) * C6]))
            fld = p.now("d_sync")
            if x2tr_done:
                p.wait("scalar", "pe", max(x2tr_done.values()))
            p.dma("scalar", f3(x2sb[:], FC, C6), fmaj(T["x2"][:, tc * C6:(tc + 1) * C6]))
            x2ld = p.now("d_scalar")
            p.wait("tensor", "d_scalar", x2ld)
            for jf in range(FC):
                tb = T[f"ps{4 + jf % 2}"]
                if (4 + jf % 2) in tr_ev:
                    p.wait("tensor", "act", tr_ev[4 + jf % 2])
                p.op("tensor", "transpose", tb[:, 0:128],
                     x2sb[:, jf * C6:(jf + 1) * C6], ident[:], inc=("pe", 1))
                tp = p.now("pe")
                p.wait("scalar", "pe", tp)
                if jf < 2 and acc_ev:
                    p.wait("scalar", "dve", max(acc_ev.values()))
                tr_ev[4 + jf % 2] = p.op("scalar", "copy",
                                         x2t[:, jf * 128:(jf + 1) * 128],
                                         tb[:, 0:128], inc=("act", 1))
            x2tr_done[tc % 2] = p.now("pe")
            x2t_act = p.now("act")

            p.wait("tensor", "d_sync", fld)
            if tc == 0:
                p.wait("tensor", "d_dwq", lad_done)
            for cg in range(2):
                bank = T[f"ps{cg}"]
                if cg in acc_ev and (tc > 0 or half > 0):
                    p.wait("tensor", "dve", acc_ev[cg])
                for kc in range(FCI):
                    fh = fsht[:, tpar * FCI * C6 + kc * C6:
                              tpar * FCI * C6 + (kc + 1) * C6]
                    fl = fslt[:, tpar * FCI * C6 + kc * C6:
                              tpar * FCI * C6 + (kc + 1) * C6]
                    wslice = dwq[:, kc * HW_ + cg * 512:kc * HW_ + (cg + 1) * 512]
                    p.op("tensor", "matmul", bank[:], fh, wslice,
                         start=(kc == 0), stop=False)
                    p.op("tensor", "matmul", bank[:], fl, wslice,
                         start=False, stop=(kc == FCI - 1),
                         inc=(("pe", 1) if kc == FCI - 1 else ()))
                mme = p.now("pe")
                p.wait("vector", "pe", mme)
                p.wait("vector", "act", x2t_act)
                if st_marks:
                    p.wait("vector", "d_gpsimd", max(st_marks.values()))
                fo = half * HW_ + cg * 512
                p.op("vector", "tensor_mul", ost[:, cg * 512:(cg + 1) * 512],
                     bank[:], gbrep[:, fo:fo + 512])
                p.op("vector", "tensor_tensor", ost[:, cg * 512:(cg + 1) * 512],
                     ost[:, cg * 512:(cg + 1) * 512], x2t[:, fo:fo + 512],
                     Alu.add, inc=("dve", 1))
                acc_ev[cg] = p.now("dve")
            p.wait("gpsimd", "dve", acc_ev[1])
            st_marks[(half, tc)] = p.dma(
                "gpsimd", T["out"][tc * C6:(tc + 1) * C6, half * HW_:(half + 1) * HW_],
                ost[:])
            fs_use = p.now("pe")

    p.barrier(engines=("sync", "gpsimd"))



_CAUS = np.concatenate([
    np.where(np.arange(2 * GS // 2)[None, :] > np.arange(128)[:, None], -1e9, 0.0),
    np.where(np.arange(2 * GS // 2)[None, :] > 128 + np.arange(128)[:, None], -1e9, 0.0),
], axis=1).astype(np.float32)
_IDENT = np.eye(128, dtype=np.float32)


def kernel(L0_attn_x, L0_attn_Wq, L0_attn_bq, L0_attn_Wk, L0_attn_bk,
           L0_attn_Wv, L0_attn_bv, L0_ffn_gate_w, L0_ffn_up_w, L0_ffn_down_w,
           L0_ffn_gate_scale, L0_ffn_up_scale, L0_ffn_down_scale, L0_c, L0_wmod):
    from concourse.bass_utils import run_bass_kernel_spmd
    nc = build_nc(("s0", "s1", "s2", "s3", "s4", "s5", "s6"))
    com = dict(
        wq=np.ascontiguousarray(L0_attn_Wq), wk=np.ascontiguousarray(L0_attn_Wk),
        wv=np.ascontiguousarray(L0_attn_Wv),
        bq=np.ascontiguousarray(L0_attn_bq)[None],
        bk=np.ascontiguousarray(L0_attn_bk)[None],
        bv=np.ascontiguousarray(L0_attn_bv)[None],
        gw=np.ascontiguousarray(L0_ffn_gate_w), uw=np.ascontiguousarray(L0_ffn_up_w),
        dw=np.ascontiguousarray(L0_ffn_down_w),
        gsc=np.ascontiguousarray(L0_ffn_gate_scale)[None],
        usc=np.ascontiguousarray(L0_ffn_up_scale)[None],
        dsc=np.ascontiguousarray(L0_ffn_down_scale)[None],
        wmod=np.ascontiguousarray(L0_wmod),
        ident=_IDENT, caus=_CAUS,
    )
    in_maps = [dict(com, x=np.ascontiguousarray(L0_attn_x[i]),
                    c=np.ascontiguousarray(L0_c[i:i + 1])) for i in range(8)]
    res = run_bass_kernel_spmd(nc, in_maps, core_ids=list(range(8)))
    return np.stack([r["out"] for r in res.results], axis=0)
